# version 05
# baseline (speedup 1.0000x reference)
import numpy as np
import ml_dtypes
from contextlib import ExitStack
import concourse.bass as bass
import concourse.mybir as mybir
from concourse.bass_isa import ReduceOp
from concourse import library_config
from concourse.bass_utils import run_bass_kernel_spmd

F32 = mybir.dt.float32
BF16 = mybir.dt.bfloat16
FP8 = mybir.dt.float8e4
AF = mybir.ActivationFunctionType
ALU = mybir.AluOpType
AX = mybir.AxisListType

B, L, D = 4, 4096, 1024
HALF = L // 2
H, DQH, DVH = 4, 128, 256
NT = L // 128
NTO = HALF // 128
DI = 4096

SEM_NAMES = ("qs", "qt", "vv", "aa", "tt", "gg")


def build_program():
    nc = bass.Bass("TRN2", target_bir_lowering=False)

    dram = {}
    def din(name, shape, dt=F32):
        dram[name] = nc.dram_tensor(name, list(shape), dt, kind="ExternalInput").ap()
        return dram[name]
    def dscr(name, shape, dt):
        dram[name] = nc.dram_tensor(name, list(shape), dt, kind="Internal").ap()
        return dram[name]

    X = din("x_seq", [L, D])
    XO = din("x_own", [HALF, D])
    WQ = din("Wq", [D, 512]); WK = din("Wk", [D, 512]); WV = din("Wv", [D, 1024])
    WB = din("Wb", [D, 4]); WG = din("Wg", [D, 1024]); WO = din("Wo", [D, 1024])
    CW = din("conv_w", [D, 4])
    NW = din("norm_w_rep", [128, 256])
    GWS = din("gw_shard", [DI // 8, D]); UWS = din("uw_shard", [DI // 8, D])
    DWS = din("dw_shard", [D // 8, DI])
    SIN = din("s_in_rep", [128, 1]); SID = din("s_in_down_rep", [128, 1])
    MSK = din("masks", [128, 6 * 64])
    OUT = nc.dram_tensor("out_full", [L, D], F32, kind="ExternalOutput").ap()
    GS = dscr("g_scratch", [L, D], F32)
    HQS = dscr("hq_scratch", [HALF, DI], BF16)
    WMX = dscr("wmax_scratch", [128], F32)
    CCMI = dscr("cc_max_in", [128, 4], F32)
    CCMO = dscr("cc_max_out", [128, 4], F32)
    CCGI = dscr("cc_g_in", [D, 512], BF16)
    CCGO = dscr("cc_g_out", [8, D, 512], BF16)
    CCUI = dscr("cc_u_in", [D, 512], BF16)
    CCUO = dscr("cc_u_out", [8, D, 512], BF16)
    CCDI = dscr("cc_d_in", [DI, 128], BF16)
    CCDO = dscr("cc_d_out", [8, DI, 128], BF16)

    st_outer = ExitStack()
    en = st_outer.enter_context
    sem = {n: en(nc.semaphore(n)) for n in SEM_NAMES}

    s_msk = en(nc.sbuf_tensor("s_msk", [128, 6 * 64], F32))
    s_nw = en(nc.sbuf_tensor("s_nw", [128, 256], F32))
    s_cw = en(nc.sbuf_tensor("s_cw", [128, 8 * 4], F32))
    s_gs = en(nc.sbuf_tensor("s_gs", [128, 16], F32))
    s_q1 = en(nc.sbuf_tensor("s_q1", [128, 1024], F32))
    s_q2 = en(nc.sbuf_tensor("s_q2", [128, 1024], F32))
    s_q3 = en(nc.sbuf_tensor("s_q3", [128, 1024], F32))
    s_q4 = en(nc.sbuf_tensor("s_q4", [128, 1024], F32))
    s_qsg = en(nc.sbuf_tensor("s_qsg", [128, 1024], F32))
    s_qam = en(nc.sbuf_tensor("s_qam", [128, 64], F32))
    s_qs8 = en(nc.sbuf_tensor("s_qs8", [128, 64], F32))
    s_qdq = en(nc.sbuf_tensor("s_qdq", [128, 64], F32))
    s_qrd = en(nc.sbuf_tensor("s_qrd", [128, 64], F32))
    s_ln = en(nc.sbuf_tensor("s_ln", [128, 16], F32))
    s_xt = en(nc.sbuf_tensor("s_xt", [128, 1024], F32))
    s_xn = en(nc.sbuf_tensor("s_xn", [128, 1024], F32))

    ts = dict(nc=nc, dram=dram, sem=sem, OUT=OUT, X=X, XO=XO,
              s_msk=s_msk, s_nw=s_nw, s_cw=s_cw, s_gs=s_gs,
              s_q1=s_q1, s_q2=s_q2, s_q3=s_q3, s_q4=s_q4, s_qsg=s_qsg,
              s_qam=s_qam, s_qs8=s_qs8, s_qdq=s_qdq, s_qrd=s_qrd,
              s_ln=s_ln, s_xt=s_xt, s_xn=s_xn)

    st_pf = ExitStack()
    s_wT = st_pf.enter_context(nc.sbuf_tensor("s_wT", [128, 32 * 1024], BF16))
    ts.update(s_wT=s_wT)
    st_p0 = ExitStack()
    s_wnat = st_p0.enter_context(nc.sbuf_tensor("s_wnat", [128, 32 * 1024], BF16))
    s_wch = st_p0.enter_context(nc.sbuf_tensor("s_wch", [128, 1024], F32))
    s_wch2 = st_p0.enter_context(nc.sbuf_tensor("s_wch2", [128, 1024], F32))
    ts.update(s_wnat=s_wnat, s_wch=s_wch, s_wch2=s_wch2)
    st_p0.close()
    st_f = ExitStack()
    enf = st_f.enter_context
    s_xqT = enf(nc.sbuf_tensor("s_xqT", [128, 8 * 2048], BF16))
    s_xqb = enf(nc.sbuf_tensor("s_xqb", [128, 1024], BF16))
    s_wgc = enf(nc.sbuf_tensor("s_wgc", [128, 8 * 1024], BF16))
    s_wuc = enf(nc.sbuf_tensor("s_wuc", [128, 8 * 1024], BF16))
    s_sg = enf(nc.sbuf_tensor("s_sg", [128, 1024], F32))
    s_hid = enf(nc.sbuf_tensor("s_hid", [128, 1024], F32))
    s_hqb = enf(nc.sbuf_tensor("s_hqb", [128, 1024], BF16))
    s_hqT = enf(nc.sbuf_tensor("s_hqT", [128, 32 * 512], BF16))
    s_gf = enf(nc.sbuf_tensor("s_gf", [128, 1024], F32))
    ts.update(s_xqT=s_xqT, s_xqb=s_xqb, s_wgc=s_wgc, s_wuc=s_wuc, s_sg=s_sg,
              s_hid=s_hid, s_hqb=s_hqb, s_hqT=s_hqT, s_gf=s_gf)
    st_f.close()
    st_pf.close()
    s_xnT = en(nc.sbuf_tensor("s_xnT", [128, 8 * (L + 16)], BF16))
    s_xnb = en(nc.sbuf_tensor("s_xnb", [128, 1024], BF16))
    s_hT = en(nc.sbuf_tensor("s_hT", [128, 8 * 128], BF16))
    s_Wq = en(nc.sbuf_tensor("s_Wq", [128, 8 * 512], BF16))
    s_Wk = en(nc.sbuf_tensor("s_Wk", [128, 8 * 512], BF16))
    s_Wv = en(nc.sbuf_tensor("s_Wv", [128, 8 * 1024], BF16))
    s_Wb = en(nc.sbuf_tensor("s_Wb", [128, 8 * 4], BF16))
    s_Wg = en(nc.sbuf_tensor("s_Wg", [128, 8 * 1024], BF16))
    s_Wo = en(nc.sbuf_tensor("s_Wo", [128, 8 * 1024], BF16))
    s_ktm = en(nc.sbuf_tensor("s_ktm", [128, 512], BF16))
    s_v = en(nc.sbuf_tensor("s_v", [128, 1024], BF16))
    s_g = en(nc.sbuf_tensor("s_g", [128, 1024], BF16))
    s_qT = en(nc.sbuf_tensor("s_qT", [128, 512], BF16))
    s_kT = en(nc.sbuf_tensor("s_kT", [128, 512], BF16))
    s_beta = en(nc.sbuf_tensor("s_beta", [128, 4], F32))
    s_st = en(nc.sbuf_tensor("s_st", [128, 24], F32))
    s_Sbf = en(nc.sbuf_tensor("s_Sbf", [128, 1024], BF16))
    s_Sf = en(nc.sbuf_tensor("s_Sf", [128, 1024], F32))
    s_uf = en(nc.sbuf_tensor("s_uf", [128, 1024], F32))
    s_ub = en(nc.sbuf_tensor("s_ub", [128, 1024], BF16))
    s_C = en(nc.sbuf_tensor("s_C", [128, 256], BF16))
    s_dg = en(nc.sbuf_tensor("s_dg", [128, 256], BF16))
    s_At = en(nc.sbuf_tensor("s_At", [128, 256], BF16))
    s_AtT = en(nc.sbuf_tensor("s_AtT", [128, 256], BF16))
    s_pw = en(nc.sbuf_tensor("s_pw", [128, 9 * 256], BF16))
    s_Lt = en(nc.sbuf_tensor("s_Lt", [128, 256], BF16))
    s_o = en(nc.sbuf_tensor("s_o", [128, 1024], F32))
    s_gtd = en(nc.sbuf_tensor("s_gtd", [128, 1024], BF16))
    s_gTt = en(nc.sbuf_tensor("s_gTt", [128, 8 * 128], BF16))
    s_gt = en(nc.sbuf_tensor("s_gt", [128, 1024], F32))
    ts.update(s_xnT=s_xnT, s_xnb=s_xnb, s_hT=s_hT, s_Wq=s_Wq, s_Wk=s_Wk,
              s_Wv=s_Wv, s_Wb=s_Wb, s_Wg=s_Wg, s_Wo=s_Wo, s_ktm=s_ktm,
              s_v=s_v, s_g=s_g, s_qT=s_qT, s_kT=s_kT, s_beta=s_beta,
              s_st=s_st, s_Sbf=s_Sbf, s_uf=s_uf, s_ub=s_ub, s_C=s_C,
              s_dg=s_dg, s_At=s_At, s_AtT=s_AtT, s_pw=s_pw, s_Lt=s_Lt,
              s_o=s_o, s_Sf=s_Sf, s_gtd=s_gtd, s_gTt=s_gTt, s_gt=s_gt)
    p_S = en(nc.psum_tensor("p_S", [128, 1024], F32))
    p_A = en(nc.psum_tensor("p_A", [128, 1024], F32))
    p_B = en(nc.psum_tensor("p_B", [128, 512], F32))
    p_C = en(nc.psum_tensor("p_C", [128, 256], F32))
    p_D = en(nc.psum_tensor("p_D", [128, 1024], F32))
    ts.update(p_S=p_S, p_A=p_A, p_B=p_B, p_C=p_C, p_D=p_D)

    with nc.Block() as block:
        @block.sync
        def _(eng):
            emit_all("sync", eng, ts)

        @block.scalar
        def _(eng):
            emit_all("scalar", eng, ts)

        @block.vector
        def _(eng):
            emit_all("vector", eng, ts)

        @block.tensor
        def _(eng):
            emit_all("tensor", eng, ts)

        @block.gpsimd
        def _(eng):
            emit_all("gpsimd", eng, ts)

    st_outer.close()
    return nc



class Emit:
    def __init__(self, which, eng, ts):
        self.w = which
        self.e = eng
        self.ts = ts
        self.sem = ts["sem"]
        self.C = {n: 0 for n in SEM_NAMES}

    def op(self, eng_name, fn, inc=None, incby=1):
        i = fn(self.e) if self.w == eng_name else None
        if inc is not None:
            self.C[inc] += incby
            if i is not None:
                i.then_inc(self.sem[inc], incby)
        return i

    def dma(self, queue, out, in_):
        eng_name = "sync" if queue == "qs" else "scalar"
        if self.w == eng_name:
            self.e.dma_start(out=out, in_=in_).then_inc(self.sem[queue], 16)
        self.C[queue] += 16

    def dmaT(self, out, in_):
        if self.w == "scalar":
            self.e.dma_start_transpose(out=out, in_=in_).then_inc(self.sem["qt"], 16)
        self.C["qt"] += 16

    def wait(self, eng_name, sems):
        for s in sems:
            if self.w == eng_name:
                self.e.wait_ge(self.sem[s], self.C[s])
            elif s == "qs" and self.w == "sync" and eng_name != "sync":
                self.e.wait_ge(self.sem[s], self.C[s])
            elif s == "qt" and self.w == "scalar" and eng_name != "scalar":
                self.e.wait_ge(self.sem[s], self.C[s])

    def drain(self, eng_name):
        if self.w == eng_name:
            self.e.drain()

    def barrier(self):
        for s in SEM_NAMES:
            self.e.wait_ge(self.sem[s], self.C[s])


def emit_all(which, eng, ts):
    em = Emit(which, eng, ts)
    phase_p0(em, ts)
    em.barrier()
    phase_f(em, ts)
    em.barrier()
    phase_a(em, ts)
    em.wait("sync", ("qs",))
    em.wait("scalar", ("qt",))



def q3v(ap, nb):
    return ap.rearrange("p (b k) -> p b k", k=16)


def bcast16(ap, nb):
    return ap.rearrange("p (b o) -> p b o", o=1).broadcast_to((128, nb, 16))


def emit_quant(em, ts, src_ap, dst_ap, ncols, gcol):
    v, a = "vector", "scalar"
    gs = ts["s_gs"]
    nb = ncols // 16
    q1, q2, q3, q4, sg = (ts["s_q1"][:, 0:ncols], ts["s_q2"][:, 0:ncols],
                          ts["s_q3"][:, 0:ncols], ts["s_q4"][:, 0:ncols],
                          ts["s_qsg"][:, 0:ncols])
    am, s8, dq, rd = (ts["s_qam"][:, 0:nb], ts["s_qs8"][:, 0:nb],
                      ts["s_qdq"][:, 0:nb], ts["s_qrd"][:, 0:nb])
    zt = ts["s_q2"][:, 0:nb]
    pt = ts["s_q2"][:, nb:2 * nb]
    ct = ts["s_q2"][:, 2 * nb:3 * nb]
    vt = ts["s_q3"][:, 0:nb]
    ut = ts["s_q3"][:, nb:2 * nb]
    em.op(v, lambda e: e.tensor_reduce(out=am, in_=q3v(src_ap, nb), axis=AX.X,
                                       op=ALU.max, apply_absolute_value=True))
    em.drain(v)
    em.op(v, lambda e: e.tensor_scalar(out=zt, in0=am, scalar1=gs[:, gcol + 1:gcol + 2],
                                       scalar2=224.0, op0=ALU.mult, op1=ALU.min))
    em.drain(v)
    em.op(v, lambda e: e.memset(pt, 2.0 ** -7))
    em.drain(v)
    for k in (8, 4, 2, 1):
        em.op(v, lambda e, k=k: e.tensor_scalar(out=ct, in0=pt, scalar1=float(2 ** k),
                                                scalar2=None, op0=ALU.mult))
        em.drain(v)
        em.op(v, lambda e: e.tensor_tensor(out=ct, in0=ct, in1=zt, op=ALU.is_le))
        em.drain(v)
        em.op(v, lambda e, k=k: e.tensor_scalar(out=ct, in0=ct,
                                                scalar1=float(2 ** k - 1),
                                                scalar2=1.0, op0=ALU.mult, op1=ALU.add))
        em.drain(v)
        em.op(v, lambda e: e.tensor_tensor(out=pt, in0=pt, in1=ct, op=ALU.mult))
        em.drain(v)
    em.op(v, lambda e: e.reciprocal(out=ct, in_=pt))
    em.drain(v)
    em.op(v, lambda e: e.scalar_tensor_tensor(out=ut, in0=zt, scalar=8.0, in1=ct,
                                              op0=ALU.mult, op1=ALU.mult))
    em.drain(v)
    em.op(v, lambda e: e.memset(vt, 0.0))
    em.drain(v)
    for k in (16, 8, 4, 2, 1):
        em.op(v, lambda e, k=k: e.scalar_tensor_tensor(out=ct, in0=vt,
                                                       scalar=float(k) - 0.5, in1=ut,
                                                       op0=ALU.add, op1=ALU.is_le))
        em.drain(v)
        em.op(v, lambda e, k=k: e.scalar_tensor_tensor(out=vt, in0=ct,
                                                       scalar=float(k), in1=vt,
                                                       op0=ALU.mult, op1=ALU.add))
        em.drain(v)
    em.op(v, lambda e: e.scalar_tensor_tensor(out=s8, in0=vt, scalar=0.125, in1=pt,
                                              op0=ALU.mult, op1=ALU.mult))
    em.drain(v)
    em.op(v, lambda e: e.tensor_scalar(out=dq, in0=s8, scalar1=gs[:, gcol + 2:gcol + 3],
                                       scalar2=None, op0=ALU.mult))
    em.op(v, lambda e: e.tensor_scalar(out=rd, in0=s8, scalar1=1e-30,
                                       scalar2=None, op0=ALU.max))
    em.drain(v)
    em.op(v, lambda e: e.reciprocal(out=rd, in_=rd))
    em.drain(v)
    em.op(v, lambda e: e.tensor_scalar(out=rd, in0=rd, scalar1=gs[:, gcol:gcol + 1],
                                       scalar2=None, op0=ALU.mult))
    em.drain(v)
    em.op(v, lambda e: e.tensor_tensor(out=q3v(q1, nb), in0=q3v(src_ap, nb),
                                       in1=bcast16(rd, nb), op=ALU.mult),
          inc="vv")
    em.wait(a, ("vv",))
    em.op(a, lambda e: e.activation(out=sg, in_=q1, func=AF.Sign))
    em.op(a, lambda e: e.activation(out=q2, in_=q1, func=AF.Abs), inc="aa")
    em.wait(v, ("aa",))
    em.op(v, lambda e: e.tensor_scalar(out=q3, in0=q2, scalar1=4.0, scalar2=2.0,
                                       op0=ALU.is_ge, op1=ALU.subtract))
    em.op(v, lambda e: e.tensor_scalar(out=q4, in0=q2, scalar1=8.0, scalar2=2.0,
                                       op0=ALU.is_ge, op1=ALU.subtract))
    em.drain(v)
    em.op(v, lambda e: e.tensor_tensor(out=q3, in0=q3, in1=q4, op=ALU.mult))
    em.drain(v)
    em.op(v, lambda e: e.scalar_tensor_tensor(out=q3, in0=q2, scalar=12.0, in1=q3,
                                              op0=ALU.min, op1=ALU.mult))
    em.drain(v)
    em.op(v, lambda e: e.tensor_scalar(out=q4, in0=q3, scalar1=2.0, scalar2=None,
                                       op0=ALU.is_ge))
    em.drain(v)
    for th in (6.0, 10.0, 14.0):
        em.op(v, lambda e, th=th: e.scalar_tensor_tensor(out=q4, in0=q3, scalar=th,
                                                         in1=q4, op0=ALU.is_ge,
                                                         op1=ALU.add))
        em.drain(v)
    em.op(v, lambda e: e.tensor_scalar(out=q3, in0=q2, scalar1=4.0, scalar2=1.0,
                                       op0=ALU.is_ge, op1=ALU.add))
    em.op(v, lambda e: e.tensor_scalar(out=q2, in0=q2, scalar1=8.0, scalar2=1.0,
                                       op0=ALU.is_ge, op1=ALU.add))
    em.drain(v)
    em.op(v, lambda e: e.tensor_tensor(out=q3, in0=q3, in1=q2, op=ALU.mult))
    em.drain(v)
    em.op(v, lambda e: e.tensor_tensor(out=q4, in0=q4, in1=q3, op=ALU.mult))
    em.drain(v)
    em.op(v, lambda e: e.tensor_tensor(out=q3v(q4, nb), in0=q3v(q4, nb),
                                       in1=bcast16(dq, nb), op=ALU.mult))
    em.drain(v)
    em.op(v, lambda e: e.tensor_tensor(out=dst_ap, in0=q4, in1=sg, op=ALU.mult),
          inc="vv")
    em.drain(v)


def emit_ln(em, ts, x_ap, xn_ap):
    v, a = "vector", "scalar"
    ln = ts["s_ln"]
    em.op(v, lambda e: e.bn_stats(out=ln[:, 0:6], in_=x_ap[:, 0:512]))
    em.op(v, lambda e: e.bn_stats(out=ln[:, 6:12], in_=x_ap[:, 512:1024]))
    em.drain(v)
    em.op(v, lambda e: e.bn_aggr(out=ln[:, 12:14], in_=ln[:, 0:12]))
    em.drain(v)
    em.op(v, lambda e: e.tensor_scalar(out=ln[:, 14:15], in0=ln[:, 13:14],
                                       scalar1=1e-5, scalar2=None, op0=ALU.add))
    em.drain(v)
    em.op(v, lambda e: e.reciprocal(out=ln[:, 14:15], in_=ln[:, 14:15]), inc="vv")
    em.wait(a, ("vv",))
    em.op(a, lambda e: e.activation(out=ln[:, 15:16], in_=ln[:, 14:15],
                                    func=AF.Sqrt), inc="aa")
    em.wait(v, ("aa",))
    em.op(v, lambda e: e.tensor_scalar(out=xn_ap, in0=x_ap, scalar1=ln[:, 12:13],
                                       scalar2=None, op0=ALU.subtract))
    em.drain(v)
    em.op(v, lambda e: e.tensor_scalar(out=xn_ap, in0=xn_ap, scalar1=ln[:, 15:16],
                                       scalar2=None, op0=ALU.mult), inc="vv")
    em.drain(v)



def phase_p0(em, ts):
    v, a, t, s, g = "vector", "scalar", "tensor", "sync", "gpsimd"
    dram = ts["dram"]
    nc_gs = ts["s_gs"]
    em.dma("qs", ts["s_msk"][:], dram["masks"][:, :])
    em.dma("qs", ts["s_nw"][:], dram["norm_w_rep"][:, :])
    em.dma("qs", nc_gs[:, 0:1], dram["s_in_rep"][:, :])
    em.dma("qs", nc_gs[:, 3:4], dram["s_in_down_rep"][:, :])
    cw3 = ts["s_cw"][:].rearrange("p (c i) -> p c i", i=4)
    em.dma("qs", cw3, dram["conv_w"][:, :].rearrange("(c p) i -> p c i", p=128))
    em.wait(v, ("qs",))
    for c in (0, 3):
        em.op(v, lambda e, c=c: e.tensor_scalar(out=nc_gs[:, c + 1:c + 2],
                                                in0=nc_gs[:, c:c + 1],
                                                scalar1=1.0 / 12.0, scalar2=None,
                                                op0=ALU.mult))
        em.op(v, lambda e, c=c: e.tensor_copy(out=nc_gs[:, c + 2:c + 3],
                                              in_=nc_gs[:, c:c + 1]))
        em.drain(v)
        em.op(v, lambda e, c=c: e.reciprocal(out=nc_gs[:, c + 2:c + 3],
                                             in_=nc_gs[:, c + 2:c + 3]))
        em.drain(v)

    em.op(v, lambda e: e.memset(ts["s_q2"][:, 0:4], 0.0), inc="vv")
    em.drain(v)
    for wi, (wname, nch, ncols) in enumerate([("gw_shard", 4, 1024),
                                              ("uw_shard", 4, 1024),
                                              ("dw_shard", 1, 4096)]):
        whbm = dram[wname]
        em.op(v, lambda e: e.memset(nc_gs[:, 15:16], 0.0), inc="vv")
        nparts = nch * (ncols // 1024)
        for pp in range(nparts):
            buf = ts["s_wch"] if pp % 2 == 0 else ts["s_wch2"]
            em.wait(s, ("vv",))
            if ncols == 1024:
                em.dma("qs", buf[:, 0:1024], whbm[pp * 128:(pp + 1) * 128, :])
            else:
                em.dma("qs", buf[:, 0:1024], whbm[0:128, pp * 1024:(pp + 1) * 1024])
            em.wait(v, ("qs",))
            em.op(v, lambda e, buf=buf: e.tensor_reduce(
                out=ts["s_qam"][:, 0:1], in_=buf[:, 0:1024], axis=AX.X,
                op=ALU.max, apply_absolute_value=True))
            em.drain(v)
            em.op(v, lambda e: e.tensor_tensor(
                out=nc_gs[:, 15:16], in0=nc_gs[:, 15:16],
                in1=ts["s_qam"][:, 0:1], op=ALU.max), inc="vv")
            em.drain(v)
        em.wait(s, ("vv",))
        em.dma("qs", dram["wmax_scratch"][:].rearrange("(p o) -> p o", o=1),
               nc_gs[:, 15:16])
        em.wait(s, ("qs",))
        em.dma("qs", ts["s_q3"][0:1, 0:128],
               dram["wmax_scratch"][:].rearrange("(o f) -> o f", o=1))
        em.wait(v, ("qs",))
        em.op(v, lambda e: e.tensor_reduce(out=ts["s_q3"][0:1, 128:129],
                                           in_=ts["s_q3"][0:1, 0:128], axis=AX.X,
                                           op=ALU.max, apply_absolute_value=True))
        em.drain(v)
        em.op(v, lambda e: e.tensor_copy(
            out=ts["s_q3"][0:1, 0:128],
            in_=ts["s_q3"][0:1, 128:129].broadcast_to((1, 128))), inc="vv")
        em.drain(v)
        em.wait(s, ("vv",))
        em.dma("qs", dram["wmax_scratch"][:].rearrange("(o f) -> o f", o=1),
               ts["s_q3"][0:1, 0:128])
        em.wait(s, ("qs",))
        em.dma("qs", ts["s_q2"][:, wi:wi + 1],
               dram["wmax_scratch"][:].rearrange("(p o) -> p o", o=1))
    em.wait(s, ("qs",))
    em.dma("qs", dram["cc_max_in"][:, :], ts["s_q2"][:, 0:4])
    em.wait(g, ("qs",))
    em.op(g, lambda e: e.collective_compute(
        "AllReduce", ALU.max, replica_groups=[list(range(8))],
        ins=[dram["cc_max_in"][:, :]], outs=[dram["cc_max_out"][:, :]]),
        inc="gg")
    em.wait(s, ("gg",))
    em.dma("qs", ts["s_q2"][:, 4:8], dram["cc_max_out"][:, :])
    em.wait(v, ("qs",))
    for wi in range(3):
        gcol = 6 + 3 * wi
        em.op(v, lambda e, wi=wi, gcol=gcol: e.reciprocal(
            out=nc_gs[:, gcol:gcol + 1], in_=ts["s_q2"][:, 4 + wi:5 + wi]))
        em.drain(v)
        em.op(v, lambda e, gcol=gcol: e.tensor_scalar(
            out=nc_gs[:, gcol:gcol + 1], in0=nc_gs[:, gcol:gcol + 1],
            scalar1=2688.0, scalar2=None, op0=ALU.mult))
        em.drain(v)
        em.op(v, lambda e, gcol=gcol: e.tensor_scalar(
            out=nc_gs[:, gcol + 1:gcol + 2], in0=nc_gs[:, gcol:gcol + 1],
            scalar1=1.0 / 12.0, scalar2=None, op0=ALU.mult))
        em.op(v, lambda e, gcol=gcol: e.tensor_copy(
            out=nc_gs[:, gcol + 2:gcol + 3], in_=nc_gs[:, gcol:gcol + 1]))
        em.drain(v)
        em.op(v, lambda e, gcol=gcol: e.reciprocal(
            out=nc_gs[:, gcol + 2:gcol + 3], in_=nc_gs[:, gcol + 2:gcol + 3]),
            inc="vv")
        em.drain(v)
    for wi, (wname, nch, ncols, cci, cco) in enumerate(
            [("gw_shard", 4, 1024, "cc_g_in", "cc_g_out"),
             ("uw_shard", 4, 1024, "cc_u_in", "cc_u_out"),
             ("dw_shard", 1, 4096, "cc_d_in", "cc_d_out")]):
        whbm = dram[wname]
        gcol = 6 + 3 * wi
        em.wait(v, ("qt",))
        nparts = nch * (ncols // 1024)
        for pp in range(nparts):
            buf = ts["s_wch"] if pp % 2 == 0 else ts["s_wch2"]
            em.wait(s, ("vv",))
            if ncols == 1024:
                em.dma("qs", buf[:, 0:1024], whbm[pp * 128:(pp + 1) * 128, :])
            else:
                em.dma("qs", buf[:, 0:1024], whbm[0:128, pp * 1024:(pp + 1) * 1024])
            em.wait(v, ("qs",))
            dst = ts["s_wnat"][:, pp * 1024:(pp + 1) * 1024]
            emit_quant(em, ts, buf[:, 0:1024], dst, 1024, gcol)
        em.wait("scalar", ("vv",))
        if ncols == 1024:
            wt3 = ts["s_wT"][:, 0:4096].rearrange("p (hc ch r) -> p hc ch r",
                                                  hc=8, ch=4, r=128)
            for ch in range(4):
                em.dmaT(wt3[:, :, ch, :], ts["s_wnat"][:, ch * 1024:(ch + 1) * 1024])
        else:
            wt3 = ts["s_wT"][:, 0:4096].rearrange("p (ic r) -> p ic r", r=128)
            em.dmaT(wt3, ts["s_wnat"][:, 0:4096])
        em.wait("sync", ("qt",))
        if ncols == 1024:
            em.dma("qs", dram[cci][:, :].rearrange("(hc pp) n -> pp hc n", pp=128),
                   ts["s_wT"][:, 0:4096].rearrange("p (hc n) -> p hc n", n=512))
        else:
            em.dma("qs", dram[cci][:, :].rearrange("(ic pp) n -> pp ic n", pp=128),
                   ts["s_wT"][:, 0:4096].rearrange("p (ic n) -> p ic n", n=128))
        em.wait(g, ("qs",))
        em.op(g, lambda e, cci=cci, cco=cco: e.collective_compute(
            "AllGather", ALU.bypass, replica_groups=[list(range(8))],
            ins=[dram[cci][:, :]], outs=[dram[cco][:, :, :]]), inc="gg")



def phase_f(em, ts):
    v, a, t, s = "vector", "scalar", "tensor", "sync"
    dram = ts["dram"]
    XO = ts["XO"]
    for tt in range(NTO):
        em.wait(s, ("vv",))
        em.dma("qs", ts["s_xt"][:], XO[tt * 128:(tt + 1) * 128, :])
        em.wait(v, ("qs",))
        emit_ln(em, ts, ts["s_xt"][:], ts["s_xn"][:])
        em.wait(v, ("qt",))
        emit_quant(em, ts, ts["s_xn"][:], ts["s_xqb"][:, 0:1024], 1024, 0)
        em.wait("scalar", ("vv",))
        xt3 = ts["s_xqT"][:].rearrange("p (hch t) -> p hch t", hch=8)[
            :, :, tt * 128: tt * 128 + 128]
        em.dmaT(xt3, ts["s_xqb"][:, 0:1024])
    em.wait(t, ("qt",))
    em.wait(v, ("qt",))
    em.wait(s, ("gg",))
    for oc in range(4):
        em.wait(s, ("tt",))
        for blk2 in range(2):
            r = 2 * oc + blk2
            em.dma("qs",
                   ts["s_wgc"][:].rearrange("p (c n) -> p c n", n=1024)[
                       :, :, blk2 * 512:(blk2 + 1) * 512],
                   dram["cc_g_out"][r, :, :].rearrange("(c p) n -> p c n", p=128))
            em.dma("qs",
                   ts["s_wuc"][:].rearrange("p (c n) -> p c n", n=1024)[
                       :, :, blk2 * 512:(blk2 + 1) * 512],
                   dram["cc_u_out"][r, :, :].rearrange("(c p) n -> p c n", p=128))
        em.wait(t, ("qs",))
        for tt in range(NTO):
            em.wait(t, ("vv", "aa"))
            for ch in range(8):
                for hf in range(2):
                    em.op(t, lambda e, ch=ch, tt=tt, hf=hf: e.matmul(
                        out=ts["p_A"][:, hf * 512:(hf + 1) * 512],
                        lhsT=ts["s_xqT"][:, ch * 2048 + tt * 128: ch * 2048 + (tt + 1) * 128],
                        rhs=ts["s_wgc"][:, ch * 1024 + hf * 512: ch * 1024 + (hf + 1) * 512],
                        start=(ch == 0), stop=(ch == 7)))
            for ch in range(8):
                for hf in range(2):
                    em.op(t, lambda e, ch=ch, tt=tt, hf=hf: e.matmul(
                        out=ts["p_D"][:, hf * 512:(hf + 1) * 512],
                        lhsT=ts["s_xqT"][:, ch * 2048 + tt * 128: ch * 2048 + (tt + 1) * 128],
                        rhs=ts["s_wuc"][:, ch * 1024 + hf * 512: ch * 1024 + (hf + 1) * 512],
                        start=(ch == 0), stop=(ch == 7)),
                        inc="tt" if (ch == 7 and hf == 1) else None)
            em.wait(a, ("tt",))
            em.op(a, lambda e: e.activation(out=ts["s_sg"][:], in_=ts["p_A"][:],
                                            func=AF.Silu), inc="aa")
            em.wait(v, ("aa",))
            em.op(v, lambda e: e.tensor_tensor(out=ts["s_hid"][:], in0=ts["s_sg"][:],
                                               in1=ts["p_D"][:], op=ALU.mult), inc="vv")
            em.drain(v)
            em.wait(v, ("qs",))
            emit_quant(em, ts, ts["s_hid"][:], ts["s_hqb"][:], 1024, 3)
            em.wait(s, ("vv",))
            em.dma("qs", dram["hq_scratch"][tt * 128:(tt + 1) * 128,
                                            oc * 1024:(oc + 1) * 1024],
                   ts["s_hqb"][:])
    em.wait(s, ("gg", "qt"))
    for r in range(8):
        em.dma("qs",
               ts["s_wT"][:].rearrange("p (ic n) -> p ic n", n=1024)[
                   :, :, r * 128:(r + 1) * 128],
               dram["cc_d_out"][r, :, :].rearrange("(ic p) n -> p ic n", p=128))
    em.wait(t, ("qs",))
    for tb in range(4):
        em.wait("scalar", ("qs",))
        outT = ts["s_hqT"][:].rearrange("p (ic r) -> p ic r", r=512)
        em.dmaT(outT, dram["hq_scratch"][tb * 512:(tb + 1) * 512, :])
        em.wait(t, ("qt",))
        for tsub in range(4):
            tt = tb * 4 + tsub
            em.wait(t, ("aa",))
            for ic in range(32):
                for half in range(2):
                    em.op(t, lambda e, ic=ic, half=half, tsub=tsub: e.matmul(
                        out=ts["p_A"][:, half * 512:(half + 1) * 512],
                        lhsT=ts["s_hqT"][:, ic * 512 + tsub * 128: ic * 512 + (tsub + 1) * 128],
                        rhs=ts["s_wT"][:, ic * 1024 + half * 512: ic * 1024 + (half + 1) * 512],
                        start=(ic == 0), stop=(ic == 31)),
                        inc="tt" if (ic == 31 and half == 1) else None)
            em.wait(a, ("tt", "qs"))
            em.op(a, lambda e: e.activation(out=ts["s_gf"][:], in_=ts["p_A"][:],
                                            func=AF.Silu), inc="aa")
            em.wait(s, ("aa",))
            em.dma("qs", dram["g_scratch"][tt * 128:(tt + 1) * 128, :], ts["s_gf"][:])
            em.dma("qs", dram["g_scratch"][HALF + tt * 128:HALF + (tt + 1) * 128, :],
                   ts["s_gf"][:])



def phase_a(em, ts):
    v, a, t, s = "vector", "scalar", "tensor", "sync"
    dram = ts["dram"]
    X = ts["X"]
    msk = ts["s_msk"]
    trilS = msk[:, 0:64]
    triuI = msk[:, 64:128]
    id4 = msk[:, 128:384]
    LP = L + 16

    for wname, sbname, dout in [("Wq", "s_Wq", 512), ("Wk", "s_Wk", 512),
                                ("Wv", "s_Wv", 1024), ("Wb", "s_Wb", 4),
                                ("Wg", "s_Wg", 1024), ("Wo", "s_Wo", 1024)]:
        for ch in range(8):
            em.wait(s, ("vv",))
            em.dma("qs", ts["s_xt"][:, 0:dout],
                   dram[wname][ch * 128:(ch + 1) * 128, :])
            em.wait(v, ("qs",))
            em.op(v, lambda e, sbname=sbname, ch=ch, dout=dout: e.tensor_copy(
                out=ts[sbname][:, ch * dout:(ch + 1) * dout],
                in_=ts["s_xt"][:, 0:dout]), inc="vv")
            em.drain(v)

    em.op(v, lambda e: e.memset(ts["s_Sbf"][:], 0.0))
    em.op(v, lambda e: e.memset(ts["s_Sf"][:], 0.0))
    for ch in range(8):
        em.op(v, lambda e, ch=ch: e.memset(ts["s_xnT"][:, ch * LP:ch * LP + 16], 0.0))
    em.op(v, lambda e: e.memset(ts["s_uf"][:], 0.0), inc="vv")

    for tt in range(NT):
        em.wait(s, ("vv",))
        em.dma("qs", ts["s_xt"][:], X[tt * 128:(tt + 1) * 128, :])
        em.wait(v, ("qs",))
        emit_ln(em, ts, ts["s_xt"][:], ts["s_xn"][:])
        em.wait(v, ("qt",))
        em.op(v, lambda e: e.tensor_copy(
            out=ts["s_xnb"][:, 0:1024], in_=ts["s_xn"][:]), inc="vv")
        em.drain(v)
        em.wait("scalar", ("vv",))
        nt3 = ts["s_xnT"][:].rearrange("p (dch t) -> p dch t", dch=8)[
            :, :, 16 + tt * 128: 16 + tt * 128 + 128]
        em.dmaT(nt3, ts["s_xnb"][:, 0:1024])
    em.wait(v, ("qt",))
    em.wait(t, ("qt",))

    for tt in range(NT):
        for ch in range(8):
            base = ch * LP + 13 + tt * 128
            cw = ts["s_cw"]
            em.wait(v, ("aa",))
            em.op(v, lambda e, ch=ch, base=base: e.tensor_scalar(
                out=ts["s_q1"][:, 0:128], in0=ts["s_xnT"][:, base + 0: base + 128],
                scalar1=cw[:, ch * 4 + 0:ch * 4 + 1], scalar2=None, op0=ALU.mult))
            em.drain(v)
            for i in (1, 2, 3):
                em.op(v, lambda e, ch=ch, base=base, i=i: e.scalar_tensor_tensor(
                    out=ts["s_q1"][:, 0:128], in0=ts["s_xnT"][:, base + i: base + i + 128],
                    scalar=cw[:, ch * 4 + i:ch * 4 + i + 1], in1=ts["s_q1"][:, 0:128],
                    op0=ALU.mult, op1=ALU.add))
                em.drain(v)
            em.op(v, lambda e: e.nop(), inc="vv")
            em.wait(a, ("vv",))
            em.op(a, lambda e, ch=ch: e.activation(
                out=ts["s_hT"][:, ch * 128:(ch + 1) * 128], in_=ts["s_q1"][:, 0:128],
                func=AF.Silu), inc="aa")
        em.wait(t, ("aa", "vv"))

        for ch in range(8):
            hsl = ts["s_hT"][:, ch * 128:(ch + 1) * 128]
            em.op(t, lambda e, ch=ch, hsl=hsl: e.matmul(
                out=ts["p_A"][:, 0:512], lhsT=hsl,
                rhs=ts["s_Wq"][:, ch * 512:(ch + 1) * 512],
                start=(ch == 0), stop=(ch == 7)))
            em.op(t, lambda e, ch=ch, hsl=hsl: e.matmul(
                out=ts["p_A"][:, 512:1024], lhsT=hsl,
                rhs=ts["s_Wk"][:, ch * 512:(ch + 1) * 512],
                start=(ch == 0), stop=(ch == 7)))
            em.op(t, lambda e, ch=ch, hsl=hsl: e.matmul(
                out=ts["p_D"][:, 0:4], lhsT=hsl,
                rhs=ts["s_Wb"][:, ch * 4:(ch + 1) * 4],
                start=(ch == 0), stop=(ch == 7)), inc="tt" if ch == 7 else None)
        em.wait(a, ("tt",))
        st = ts["s_st"]
        for h in range(4):
            em.op(a, lambda e, h=h: e.activation(
                out=ts["s_q3"][:, h * 128:(h + 1) * 128],
                in_=ts["p_A"][:, 512 + h * 128:512 + (h + 1) * 128],
                func=AF.Square, accum_out=st[:, h:h + 1]))
        for h in range(4):
            em.op(a, lambda e, h=h: e.activation(
                out=ts["s_q3"][:, 512 + h * 128:512 + (h + 1) * 128],
                in_=ts["p_A"][:, h * 128:(h + 1) * 128],
                func=AF.Square, accum_out=st[:, 4 + h:5 + h]))
        em.op(a, lambda e: e.activation(out=ts["s_beta"][:], in_=ts["p_D"][:, 0:4],
                                        func=AF.Sigmoid), inc="aa")
        em.wait(v, ("tt", "aa"))
        em.op(v, lambda e: e.tensor_copy(out=ts["s_ktm"][:], in_=ts["p_A"][:, 512:1024]),
              inc="vv")
        em.op(v, lambda e: e.tensor_scalar(out=st[:, 8:16], in0=st[:, 0:8],
                                           scalar1=1e-24, scalar2=None, op0=ALU.max),
              inc="vv")
        em.drain(v)
        em.wait(a, ("vv",))
        em.op(a, lambda e: e.activation(out=st[:, 8:16], in_=st[:, 8:16], func=AF.Sqrt),
              inc="aa")
        em.wait(v, ("aa",))
        em.op(v, lambda e: e.tensor_scalar(out=st[:, 8:16], in0=st[:, 8:16],
                                           scalar1=1e-12, scalar2=None, op0=ALU.max))
        em.drain(v)
        em.op(v, lambda e: e.reciprocal(out=st[:, 8:16], in_=st[:, 8:16]))
        em.drain(v)
        em.op(v, lambda e: e.tensor_tensor(out=st[:, 16:20], in0=st[:, 8:12],
                                           in1=ts["s_beta"][:], op=ALU.mult))
        em.drain(v)
        em.op(v, lambda e: e.tensor_copy(out=st[:, 20:24], in_=st[:, 16:20]))
        em.drain(v)
        em.op(v, lambda e: e.tensor_tensor(out=st[:, 16:20], in0=st[:, 16:20],
                                           in1=st[:, 8:12], op=ALU.mult), inc="vv")
        em.drain(v)

        em.wait(t, ("vv", "aa"))
        for h in range(4):
            for ch in range(8):
                em.op(t, lambda e, h=h, ch=ch: e.matmul(
                    out=ts["p_A"][:, h * 128:(h + 1) * 128],
                    lhsT=ts["s_Wq"][:, ch * 512 + h * 128: ch * 512 + (h + 1) * 128],
                    rhs=ts["s_hT"][:, ch * 128:(ch + 1) * 128],
                    start=(ch == 0), stop=(ch == 7)))
        for h in range(4):
            for ch in range(8):
                em.op(t, lambda e, h=h, ch=ch: e.matmul(
                    out=ts["p_A"][:, 512 + h * 128: 512 + (h + 1) * 128],
                    lhsT=ts["s_Wk"][:, ch * 512 + h * 128: ch * 512 + (h + 1) * 128],
                    rhs=ts["s_hT"][:, ch * 128:(ch + 1) * 128],
                    start=(ch == 0), stop=(ch == 7)),
                    inc="tt" if (h == 3 and ch == 7) else None)
        em.wait(v, ("tt",))
        em.op(v, lambda e: e.tensor_copy(out=ts["s_qT"][:], in_=ts["p_A"][:, 0:512]))
        em.op(v, lambda e: e.tensor_copy(out=ts["s_kT"][:], in_=ts["p_A"][:, 512:1024]),
              inc="vv")
        em.wait(t, ("vv",))
        for ch in range(8):
            hsl = ts["s_hT"][:, ch * 128:(ch + 1) * 128]
            for half in range(2):
                em.op(t, lambda e, ch=ch, hsl=hsl, half=half: e.matmul(
                    out=ts["p_A"][:, half * 512:(half + 1) * 512], lhsT=hsl,
                    rhs=ts["s_Wv"][:, ch * 1024 + half * 512: ch * 1024 + (half + 1) * 512],
                    start=(ch == 0), stop=(ch == 7)),
                    inc="tt" if (ch == 7 and half == 1) else None)
        em.wait(v, ("tt",))
        em.op(v, lambda e: e.tensor_copy(out=ts["s_v"][:], in_=ts["p_A"][:]), inc="vv")
        em.wait(t, ("vv",))
        for ch in range(8):
            hsl = ts["s_hT"][:, ch * 128:(ch + 1) * 128]
            for half in range(2):
                em.op(t, lambda e, ch=ch, hsl=hsl, half=half: e.matmul(
                    out=ts["p_A"][:, half * 512:(half + 1) * 512], lhsT=hsl,
                    rhs=ts["s_Wg"][:, ch * 1024 + half * 512: ch * 1024 + (half + 1) * 512],
                    start=(ch == 0), stop=(ch == 7)),
                    inc="tt" if (ch == 7 and half == 1) else None)
        em.wait(v, ("tt",))
        em.op(v, lambda e: e.tensor_copy(out=ts["s_g"][:], in_=ts["p_A"][:]), inc="vv")
        em.drain(v)

        for c in range(2):
            hb = slice(64 * c, 64 * c + 64)
            tpK = 64 * c
            emit_chunk(em, ts, tt, c, hb, tpK, trilS, triuI, id4)

        st = ts["s_st"]
        em.wait(a, ("vv", "qs"))
        for h in range(4):
            em.op(a, lambda e, h=h: e.activation(
                out=ts["s_q2"][:, h * 256:(h + 1) * 256],
                in_=ts["s_o"][:, h * 256:(h + 1) * 256],
                func=AF.Square, accum_out=st[:, h:h + 1]))
        em.op(a, lambda e: e.activation(out=ts["s_q4"][:], in_=ts["s_g"][:],
                                        func=AF.Silu), inc="aa")
        em.wait(v, ("aa",))
        em.op(v, lambda e: e.tensor_scalar(out=st[:, 4:8], in0=st[:, 0:4],
                                           scalar1=1.0 / 256.0, scalar2=1e-5,
                                           op0=ALU.mult, op1=ALU.add))
        em.drain(v)
        em.op(v, lambda e: e.reciprocal(out=st[:, 4:8], in_=st[:, 4:8]), inc="vv")
        em.wait(a, ("vv",))
        em.op(a, lambda e: e.activation(out=st[:, 4:8], in_=st[:, 4:8], func=AF.Sqrt),
              inc="aa")
        em.wait(v, ("aa",))
        o4 = ts["s_o"][:].rearrange("p (h f) -> p h f", f=256)
        rs4 = st[:, 4:8].rearrange("p (h o) -> p h o", o=1).broadcast_to((128, 4, 256))
        em.op(v, lambda e: e.tensor_tensor(out=o4, in0=o4, in1=rs4, op=ALU.mult))
        em.drain(v)
        nw4 = ts["s_nw"][:].rearrange("p (o f) -> p o f", o=1).broadcast_to((128, 4, 256))
        em.op(v, lambda e: e.tensor_tensor(out=o4, in0=o4, in1=nw4, op=ALU.mult))
        em.drain(v)
        em.wait(v, ("qt",))
        em.op(v, lambda e: e.tensor_tensor(out=ts["s_gtd"][:], in0=ts["s_o"][:],
                                           in1=ts["s_q4"][:], op=ALU.mult), inc="vv")
        em.drain(v)
        em.wait("scalar", ("vv",))
        outT = ts["s_gTt"][:].rearrange("p (c r) -> p c r", r=128)
        em.dmaT(outT, ts["s_gtd"][:])
        em.wait(t, ("qt", "aa", "vv"))
        for ch in range(8):
            for half in range(2):
                em.op(t, lambda e, ch=ch, half=half: e.matmul(
                    out=ts["p_A"][:, half * 512:(half + 1) * 512],
                    lhsT=ts["s_gTt"][:, ch * 128:(ch + 1) * 128],
                    rhs=ts["s_Wo"][:, ch * 1024 + half * 512: ch * 1024 + (half + 1) * 512],
                    start=(ch == 0), stop=(ch == 7)),
                    inc="tt" if (ch == 7 and half == 1) else None)
        em.wait(s, ("vv",))
        em.dma("qs", ts["s_gt"][:], dram["g_scratch"][tt * 128:(tt + 1) * 128, :])
        em.dma("qs", ts["s_xt"][:], X[tt * 128:(tt + 1) * 128, :])
        em.wait(v, ("tt", "qs"))
        em.op(v, lambda e: e.tensor_tensor(out=ts["s_q1"][:], in0=ts["p_A"][:],
                                           in1=ts["s_gt"][:], op=ALU.mult))
        em.drain(v)
        em.op(v, lambda e: e.tensor_tensor(out=ts["s_q2"][:], in0=ts["s_q1"][:],
                                           in1=ts["s_xt"][:], op=ALU.add), inc="vv")
        em.drain(v)
        em.wait(s, ("vv",))
        em.dma("qs", ts["OUT"][tt * 128:(tt + 1) * 128, :], ts["s_q2"][:])


def emit_chunk(em, ts, tt, c, hb, tpK, trilS, triuI, id4):
    v, a, t = "vector", "scalar", "tensor"
    st = ts["s_st"]
    tok0 = 64 * c
    tpO = (0, 64 * c)
    tpKK = (64 * c, 64 * c)

    em.wait(t, ("vv", "aa"))
    for h in range(4):
        kTc = ts["s_kT"][:, h * 128 + tok0: h * 128 + tok0 + 64]
        em.op(t, lambda e, h=h, kTc=kTc: e.matmul(
            out=ts["p_C"][hb, h * 64:(h + 1) * 64], lhsT=kTc, rhs=kTc,
            start=True, stop=True, tile_position=tpO),
            inc="tt" if h == 3 else None)
    em.wait(v, ("tt",))
    g3 = ts["p_C"][hb, :].rearrange("p (h f) -> p h f", f=64)
    tril3 = trilS[hb, :].rearrange("p (o f) -> p o f", o=1).broadcast_to((64, 4, 64))
    em.op(v, lambda e: e.tensor_tensor(out=ts["s_C"][hb, :].rearrange(
        "p (h f) -> p h f", f=64), in0=g3, in1=tril3, op=ALU.mult))
    em.drain(v)
    s3 = st[hb, 16:20].rearrange("p (h o) -> p h o", o=1).broadcast_to((64, 4, 64))
    em.op(v, lambda e: e.tensor_tensor(out=ts["s_dg"][hb, :].rearrange(
        "p (h f) -> p h f", f=64), in0=id4[hb, :].rearrange("p (h f) -> p h f", f=64),
        in1=s3, op=ALU.mult))
    em.op(v, lambda e: e.tensor_tensor(out=ts["s_At"][hb, :].rearrange(
        "p (h f) -> p h f", f=64), in0=ts["s_C"][hb, :].rearrange(
        "p (h f) -> p h f", f=64), in1=s3, op=ALU.mult), inc="vv")
    em.drain(v)
    em.wait(t, ("vv",))
    for h in range(4):
        em.op(t, lambda e, h=h: e.matmul(
            out=ts["p_B"][hb, h * 64:(h + 1) * 64],
            lhsT=ts["s_C"][hb, h * 64:(h + 1) * 64],
            rhs=ts["s_dg"][hb, h * 64:(h + 1) * 64],
            start=True, stop=True, tile_position=tpKK),
            inc="tt" if h == 3 else None)
    em.wait(v, ("tt",))
    em.op(v, lambda e: e.tensor_copy(out=ts["s_AtT"][hb, :], in_=ts["p_B"][hb, 0:256]),
          inc="vv")
    em.drain(v)
    em.wait(t, ("vv",))
    for h in range(4):
        kTc = ts["s_kT"][:, h * 128 + tok0: h * 128 + tok0 + 64]
        em.op(t, lambda e, h=h, kTc=kTc: e.matmul(
            out=ts["p_A"][hb, h * 256:(h + 1) * 256], lhsT=kTc,
            rhs=ts["s_Sbf"][:, h * 256:(h + 1) * 256],
            start=True, stop=True, tile_position=tpO),
            inc="tt" if h == 3 else None)
    def pwY(Lv):
        return ts["s_pw"][hb, Lv * 512: Lv * 512 + 256]
    def pwYT(Lv):
        return ts["s_pw"][hb, Lv * 512 + 256: Lv * 512 + 512]
    prevY, prevYT = ts["s_At"], ts["s_AtT"]
    for Lv in range(4):
        em.wait(t, ("vv",))
        for h in range(4):
            hs = slice(h * 64, (h + 1) * 64)
            em.op(t, lambda e, h=h, hs=hs: e.matmul(
                out=ts["p_B"][hb, h * 64:(h + 1) * 64],
                lhsT=prevYT[hb, hs], rhs=prevY[hb, hs],
                start=True, stop=True, tile_position=tpKK))
            em.op(t, lambda e, h=h, hs=hs: e.matmul(
                out=ts["p_B"][hb, 256 + h * 64:256 + (h + 1) * 64],
                lhsT=prevY[hb, hs], rhs=prevYT[hb, hs],
                start=True, stop=True, tile_position=tpKK),
                inc="tt" if h == 3 else None)
        em.wait(v, ("tt",))
        if Lv < 3:
            em.op(v, lambda e, Lv=Lv: e.tensor_copy(
                out=ts["s_pw"][hb, Lv * 512:(Lv + 1) * 512], in_=ts["p_B"][hb, :]),
                inc="vv")
        else:
            em.op(v, lambda e, Lv=Lv: e.tensor_copy(
                out=ts["s_pw"][hb, Lv * 512:(Lv + 1) * 512], in_=ts["p_B"][hb, :]),
                inc="vv")
        em.drain(v)
        prevY, prevYT = None, None
        prevY = _PwSlice(ts, Lv, 0)
        prevYT = _PwSlice(ts, Lv, 1)
    em.wait(v, ("tt",))
    ks3 = ts["p_A"][hb, :].rearrange("p (h f) -> p h f", f=256)
    rk3 = st[hb, 8:12].rearrange("p (h o) -> p h o", o=1).broadcast_to((64, 4, 256))
    u3 = ts["s_uf"][hb, :].rearrange("p (h f) -> p h f", f=256)
    v3 = ts["s_v"][hb, :].rearrange("p (h f) -> p h f", f=256)
    em.op(v, lambda e: e.tensor_tensor(out=u3, in0=ks3, in1=rk3, op=ALU.mult))
    em.drain(v)
    em.op(v, lambda e: e.tensor_tensor(out=u3, in0=v3, in1=u3, op=ALU.subtract))
    em.drain(v)
    rkb3 = st[hb, 20:24].rearrange("p (h o) -> p h o", o=1).broadcast_to((64, 4, 256))
    ub3 = ts["s_ub"][hb, :].rearrange("p (h f) -> p h f", f=256)
    em.op(v, lambda e: e.tensor_tensor(out=ub3, in0=u3, in1=rkb3, op=ALU.mult),
          inc="vv")
    em.drain(v)
    em.wait(t, ("vv",))
    for h in range(4):
        hs = slice(h * 64, (h + 1) * 64)
        em.op(t, lambda e, h=h, hs=hs: e.matmul(
            out=ts["p_B"][hb, h * 64:(h + 1) * 64],
            lhsT=_PwSlice(ts, 3, 0)[hb, hs], rhs=_PwSlice(ts, 3, 1)[hb, hs],
            start=True, stop=True, tile_position=tpKK),
            inc="tt" if h == 3 else None)
    em.wait(v, ("tt",))
    em.op(v, lambda e: e.tensor_copy(out=ts["s_Lt"][hb, :], in_=ts["p_B"][hb, 0:256]),
          inc="vv")
    em.drain(v)
    app_seq = [(ts["s_Lt"], 1.0), (_PwSlice(ts, 3, 1), 1.0), (_PwSlice(ts, 2, 1), 1.0),
               (_PwSlice(ts, 1, 1), 1.0), (_PwSlice(ts, 0, 1), 1.0),
               (ts["s_AtT"], -1.0)]
    for lhsT_t, sgn in app_seq:
        em.wait(t, ("vv",))
        for h in range(4):
            em.op(t, lambda e, h=h, lhsT_t=lhsT_t: e.matmul(
                out=ts["p_A"][hb, h * 256:(h + 1) * 256],
                lhsT=lhsT_t[hb, h * 64:(h + 1) * 64],
                rhs=ts["s_ub"][hb, h * 256:(h + 1) * 256],
                start=True, stop=True, tile_position=tpKK),
                inc="tt" if h == 3 else None)
        em.wait(v, ("tt",))
        em.op(v, lambda e, sgn=sgn: e.scalar_tensor_tensor(
            out=ts["s_ub"][hb, :], in0=ts["p_A"][hb, :], scalar=sgn,
            in1=ts["s_ub"][hb, :], op0=ALU.mult, op1=ALU.add), inc="vv")
        em.drain(v)
    em.wait(t, ("vv",))
    for h in range(4):
        qTc = ts["s_qT"][:, h * 128 + tok0: h * 128 + tok0 + 64]
        em.op(t, lambda e, h=h, qTc=qTc: e.matmul(
            out=ts["p_D"][hb, h * 256:(h + 1) * 256], lhsT=qTc,
            rhs=ts["s_Sbf"][:, h * 256:(h + 1) * 256],
            start=(h % 2 == 0), stop=False, tile_position=tpO))
    for h in range(4):
        qTc = ts["s_qT"][:, h * 128 + tok0: h * 128 + tok0 + 64]
        kTc = ts["s_kT"][:, h * 128 + tok0: h * 128 + tok0 + 64]
        em.op(t, lambda e, h=h, qTc=qTc, kTc=kTc: e.matmul(
            out=ts["p_C"][hb, h * 64:(h + 1) * 64], lhsT=kTc, rhs=qTc,
            start=True, stop=True, tile_position=tpO),
            inc="tt" if h == 3 else None)
    em.wait(v, ("tt",))
    r3 = ts["p_C"][hb, :].rearrange("p (h f) -> p h f", f=64)
    triu3 = triuI[hb, :].rearrange("p (o f) -> p o f", o=1).broadcast_to((64, 4, 64))
    em.op(v, lambda e: e.tensor_tensor(out=ts["s_Lt"][hb, :].rearrange(
        "p (h f) -> p h f", f=64), in0=r3, in1=triu3, op=ALU.mult), inc="vv")
    em.drain(v)
    em.wait(t, ("vv",))
    for h in range(4):
        em.op(t, lambda e, h=h: e.matmul(
            out=ts["p_D"][hb, h * 256:(h + 1) * 256],
            lhsT=ts["s_Lt"][hb, h * 64:(h + 1) * 64],
            rhs=ts["s_ub"][hb, h * 256:(h + 1) * 256],
            start=False, stop=(h % 2 == 1), tile_position=tpKK),
            inc="tt" if h == 3 else None)
    for h in range(4):
        em.op(t, lambda e, h=h: e.matmul(
            out=ts["p_S"][:, h * 256:(h + 1) * 256],
            lhsT=ts["s_ktm"][hb, h * 128:(h + 1) * 128],
            rhs=ts["s_ub"][hb, h * 256:(h + 1) * 256],
            start=True, stop=True, tile_position=(64 * c, 0)),
            inc="tt" if h == 3 else None)
    em.wait(v, ("tt",))
    o3 = ts["s_o"][hb, :].rearrange("p (h f) -> p h f", f=256)
    rq3 = st[hb, 12:16].rearrange("p (h o) -> p h o", o=1).broadcast_to((64, 4, 256))
    em.op(v, lambda e: e.tensor_tensor(out=o3, in0=ts["p_D"][hb, :].rearrange(
        "p (h f) -> p h f", f=256), in1=rq3, op=ALU.mult))
    em.op(v, lambda e: e.scalar_tensor_tensor(
        out=ts["s_Sf"][:], in0=ts["p_S"][:], scalar=1.0, in1=ts["s_Sf"][:],
        op0=ALU.mult, op1=ALU.add))
    em.drain(v)
    em.op(v, lambda e: e.tensor_copy(out=ts["s_Sbf"][:], in_=ts["s_Sf"][:]), inc="vv")
    em.drain(v)


class _PwSlice:
    def __init__(self, ts, Lv, which):
        self.t = ts["s_pw"]
        self.base = Lv * 512 + which * 256
    def __getitem__(self, key):
        hb, hs = key
        if isinstance(hs, slice):
            return self.t[hb, self.base + hs.start: self.base + hs.stop]
        raise KeyError(key)



_HOST_CONSTS = None

def _host_consts():
    global _HOST_CONSTS
    if _HOST_CONSTS is None:
        tril = np.tril(np.ones((64, 64), np.float32), -1)
        triu = np.triu(np.ones((64, 64), np.float32), 0)
        ident = np.eye(64, dtype=np.float32)
        id4 = np.concatenate([ident] * 4, axis=1)
        row = np.concatenate([tril, triu, id4], axis=1)
        _HOST_CONSTS = np.concatenate([row, row], axis=0)
    return _HOST_CONSTS


def kernel(**inputs):
    x = np.ascontiguousarray(inputs["L0_attn_hidden_states"], dtype=np.float32)
    consts = _host_consts()
    nw_rep = np.ascontiguousarray(
        np.broadcast_to(inputs["L0_attn_norm_w"][None, :], (128, 256)), np.float32)
    sin_rep = np.ascontiguousarray(
        np.broadcast_to(inputs["L0_ffn_s_in"].reshape(1, 1), (128, 1)), np.float32)
    sid_rep = np.ascontiguousarray(
        np.broadcast_to(inputs["L0_ffn_s_in_down"].reshape(1, 1), (128, 1)), np.float32)
    shared = {
        "Wq": np.ascontiguousarray(inputs["L0_attn_Wq"], np.float32),
        "Wk": np.ascontiguousarray(inputs["L0_attn_Wk"], np.float32),
        "Wv": np.ascontiguousarray(inputs["L0_attn_Wv"], np.float32),
        "Wb": np.ascontiguousarray(inputs["L0_attn_Wb"], np.float32),
        "Wg": np.ascontiguousarray(inputs["L0_attn_Wg"], np.float32),
        "Wo": np.ascontiguousarray(inputs["L0_attn_Wo"], np.float32),
        "conv_w": np.ascontiguousarray(inputs["L0_attn_conv_w"], np.float32),
        "norm_w_rep": nw_rep,

        "s_in_rep": sin_rep,
        "s_in_down_rep": sid_rep,
        "masks": consts,
    }
    in_maps = []
    for c in range(8):
        b, half = c // 2, c % 2
        m = dict(shared)
        m["x_seq"] = np.ascontiguousarray(x[b])
        m["x_own"] = np.ascontiguousarray(x[b, half * HALF:(half + 1) * HALF])
        m["gw_shard"] = np.ascontiguousarray(
            inputs["L0_ffn_gate_w"][c * (DI // 8):(c + 1) * (DI // 8)], np.float32)
        m["uw_shard"] = np.ascontiguousarray(
            inputs["L0_ffn_up_w"][c * (DI // 8):(c + 1) * (DI // 8)], np.float32)
        m["dw_shard"] = np.ascontiguousarray(
            inputs["L0_ffn_down_w"][c * (D // 8):(c + 1) * (D // 8)], np.float32)
        in_maps.append(m)
    nc = build_program()
    res = run_bass_kernel_spmd(nc, in_maps, core_ids=list(range(8)))
    out = np.empty((B, L, D), np.float32)
    for c in range(8):
        b, half = c // 2, c % 2
        out[b, half * HALF:(half + 1) * HALF] = \
            res.results[c]["out_full"][half * HALF:(half + 1) * HALF]
    return out


# version 06
# speedup vs baseline: 1.0316x; 1.0316x over previous
import numpy as np
import ml_dtypes
from contextlib import ExitStack
import concourse.bass as bass
import concourse.mybir as mybir
from concourse.bass_isa import ReduceOp
from concourse import library_config
from concourse.bass_utils import run_bass_kernel_spmd

F32 = mybir.dt.float32
BF16 = mybir.dt.bfloat16
FP8 = mybir.dt.float8e4
AF = mybir.ActivationFunctionType
ALU = mybir.AluOpType
AX = mybir.AxisListType

B, L, D = 4, 4096, 1024
HALF = L // 2
H, DQH, DVH = 4, 128, 256
NT = L // 128
NTO = HALF // 128
DI = 4096

SEM_NAMES = ("qs", "qt", "vv", "aa", "tt", "gg")


def build_program():
    nc = bass.Bass("TRN2", target_bir_lowering=False)

    dram = {}
    def din(name, shape, dt=F32):
        dram[name] = nc.dram_tensor(name, list(shape), dt, kind="ExternalInput").ap()
        return dram[name]
    def dscr(name, shape, dt):
        dram[name] = nc.dram_tensor(name, list(shape), dt, kind="Internal").ap()
        return dram[name]

    X = din("x_seq", [L, D])
    XO = din("x_own", [HALF, D])
    WQ = din("Wq", [D, 512]); WK = din("Wk", [D, 512]); WV = din("Wv", [D, 1024])
    WB = din("Wb", [D, 4]); WG = din("Wg", [D, 1024]); WO = din("Wo", [D, 1024])
    CW = din("conv_w", [D, 4])
    NW = din("norm_w_rep", [128, 256])
    GWS = din("gw_shard", [DI // 8, D]); UWS = din("uw_shard", [DI // 8, D])
    DWS = din("dw_shard", [D // 8, DI])
    SIN = din("s_in_rep", [128, 1]); SID = din("s_in_down_rep", [128, 1])
    MSK = din("masks", [128, 6 * 64])
    OUT = nc.dram_tensor("out_full", [L, D], F32, kind="ExternalOutput").ap()
    GS = dscr("g_scratch", [L, D], F32)
    HQS = dscr("hq_scratch", [HALF, DI], BF16)
    WMX = dscr("wmax_scratch", [128], F32)
    CCMI = dscr("cc_max_in", [128, 4], F32)
    CCMO = dscr("cc_max_out", [128, 4], F32)
    CCGI = dscr("cc_g_in", [D, 512], BF16)
    CCGO = dscr("cc_g_out", [8, D, 512], BF16)
    CCUI = dscr("cc_u_in", [D, 512], BF16)
    CCUO = dscr("cc_u_out", [8, D, 512], BF16)
    CCDI = dscr("cc_d_in", [DI, 128], BF16)
    CCDO = dscr("cc_d_out", [8, DI, 128], BF16)

    st_outer = ExitStack()
    en = st_outer.enter_context
    sem = {n: en(nc.semaphore(n)) for n in SEM_NAMES}

    s_msk = en(nc.sbuf_tensor("s_msk", [128, 6 * 64], F32))
    s_nw = en(nc.sbuf_tensor("s_nw", [128, 256], F32))
    s_cw = en(nc.sbuf_tensor("s_cw", [128, 8 * 4], F32))
    s_gs = en(nc.sbuf_tensor("s_gs", [128, 16], F32))
    s_q1 = en(nc.sbuf_tensor("s_q1", [128, 1024], F32))
    s_q2 = en(nc.sbuf_tensor("s_q2", [128, 1024], F32))
    s_q3 = en(nc.sbuf_tensor("s_q3", [128, 1024], F32))
    s_q4 = en(nc.sbuf_tensor("s_q4", [128, 1024], F32))
    s_qsg = en(nc.sbuf_tensor("s_qsg", [128, 1024], F32))
    s_qam = en(nc.sbuf_tensor("s_qam", [128, 64], F32))
    s_qs8 = en(nc.sbuf_tensor("s_qs8", [128, 64], F32))
    s_qdq = en(nc.sbuf_tensor("s_qdq", [128, 64], F32))
    s_qrd = en(nc.sbuf_tensor("s_qrd", [128, 64], F32))
    s_ln = en(nc.sbuf_tensor("s_ln", [128, 16], F32))
    s_xt = en(nc.sbuf_tensor("s_xt", [128, 1024], F32))
    s_xn = en(nc.sbuf_tensor("s_xn", [128, 1024], F32))
    s_mh = en(nc.sbuf_tensor("s_mh", [128, 1], F32))

    ts = dict(nc=nc, dram=dram, sem=sem, OUT=OUT, X=X, XO=XO,
              s_msk=s_msk, s_nw=s_nw, s_cw=s_cw, s_gs=s_gs,
              s_q1=s_q1, s_q2=s_q2, s_q3=s_q3, s_q4=s_q4, s_qsg=s_qsg,
              s_qam=s_qam, s_qs8=s_qs8, s_qdq=s_qdq, s_qrd=s_qrd,
              s_ln=s_ln, s_xt=s_xt, s_xn=s_xn, s_mh=s_mh)

    st_pf = ExitStack()
    s_wT = st_pf.enter_context(nc.sbuf_tensor("s_wT", [128, 32 * 1024], BF16))
    ts.update(s_wT=s_wT)
    st_p0 = ExitStack()
    s_wnat = st_p0.enter_context(nc.sbuf_tensor("s_wnat", [128, 32 * 1024], BF16))
    s_wch = st_p0.enter_context(nc.sbuf_tensor("s_wch", [128, 1024], F32))
    s_wch2 = st_p0.enter_context(nc.sbuf_tensor("s_wch2", [128, 1024], F32))
    ts.update(s_wnat=s_wnat, s_wch=s_wch, s_wch2=s_wch2)
    st_p0.close()
    st_f = ExitStack()
    enf = st_f.enter_context
    s_xqT = enf(nc.sbuf_tensor("s_xqT", [128, 8 * 2048], BF16))
    s_xqb = enf(nc.sbuf_tensor("s_xqb", [128, 1024], BF16))
    s_wgc = enf(nc.sbuf_tensor("s_wgc", [128, 8 * 1024], BF16))
    s_wuc = enf(nc.sbuf_tensor("s_wuc", [128, 8 * 1024], BF16))
    s_sg = enf(nc.sbuf_tensor("s_sg", [128, 1024], F32))
    s_hid = enf(nc.sbuf_tensor("s_hid", [128, 1024], F32))
    s_hqb = enf(nc.sbuf_tensor("s_hqb", [128, 1024], BF16))
    s_hqT = enf(nc.sbuf_tensor("s_hqT", [128, 32 * 512], BF16))
    s_gf = enf(nc.sbuf_tensor("s_gf", [128, 1024], F32))
    ts.update(s_xqT=s_xqT, s_xqb=s_xqb, s_wgc=s_wgc, s_wuc=s_wuc, s_sg=s_sg,
              s_hid=s_hid, s_hqb=s_hqb, s_hqT=s_hqT, s_gf=s_gf)
    st_f.close()
    st_pf.close()
    s_xnT = en(nc.sbuf_tensor("s_xnT", [128, 8 * (L + 16)], BF16))
    s_xnb = en(nc.sbuf_tensor("s_xnb", [128, 1024], BF16))
    s_hT = en(nc.sbuf_tensor("s_hT", [128, 8 * 128], BF16))
    s_Wq = en(nc.sbuf_tensor("s_Wq", [128, 8 * 512], BF16))
    s_Wk = en(nc.sbuf_tensor("s_Wk", [128, 8 * 512], BF16))
    s_Wv = en(nc.sbuf_tensor("s_Wv", [128, 8 * 1024], BF16))
    s_Wb = en(nc.sbuf_tensor("s_Wb", [128, 8 * 4], BF16))
    s_Wg = en(nc.sbuf_tensor("s_Wg", [128, 8 * 1024], BF16))
    s_Wo = en(nc.sbuf_tensor("s_Wo", [128, 8 * 1024], BF16))
    s_qk = en(nc.sbuf_tensor("s_qk", [128, 1024], BF16))
    s_v = en(nc.sbuf_tensor("s_v", [128, 1024], BF16))
    s_g = en(nc.sbuf_tensor("s_g", [128, 1024], BF16))
    s_qkT = en(nc.sbuf_tensor("s_qkT", [128, 1024], BF16))
    s_beta = en(nc.sbuf_tensor("s_beta", [128, 4], F32))
    s_st = en(nc.sbuf_tensor("s_st", [128, 24], F32))
    s_Sbf = en(nc.sbuf_tensor("s_Sbf", [128, 1024], BF16))
    s_Sf = en(nc.sbuf_tensor("s_Sf", [128, 1024], F32))
    s_uf = en(nc.sbuf_tensor("s_uf", [128, 1024], F32))
    s_ub = en(nc.sbuf_tensor("s_ub", [128, 1024], BF16))
    s_C = en(nc.sbuf_tensor("s_C", [128, 256], BF16))
    s_dg = en(nc.sbuf_tensor("s_dg", [128, 256], BF16))
    s_At = en(nc.sbuf_tensor("s_At", [128, 256], BF16))
    s_AtT = en(nc.sbuf_tensor("s_AtT", [128, 256], BF16))
    s_pw = en(nc.sbuf_tensor("s_pw", [128, 9 * 256], BF16))
    s_Lt = en(nc.sbuf_tensor("s_Lt", [128, 256], BF16))
    s_o = en(nc.sbuf_tensor("s_o", [128, 1024], F32))
    s_gtd = en(nc.sbuf_tensor("s_gtd", [128, 1024], BF16))
    s_gTt = en(nc.sbuf_tensor("s_gTt", [128, 8 * 128], BF16))
    s_gt = en(nc.sbuf_tensor("s_gt", [128, 1024], F32))
    ts.update(s_xnT=s_xnT, s_xnb=s_xnb, s_hT=s_hT, s_Wq=s_Wq, s_Wk=s_Wk,
              s_Wv=s_Wv, s_Wb=s_Wb, s_Wg=s_Wg, s_Wo=s_Wo, s_qk=s_qk,
              s_v=s_v, s_g=s_g, s_qkT=s_qkT, s_beta=s_beta,
              s_st=s_st, s_Sbf=s_Sbf, s_uf=s_uf, s_ub=s_ub, s_C=s_C,
              s_dg=s_dg, s_At=s_At, s_AtT=s_AtT, s_pw=s_pw, s_Lt=s_Lt,
              s_o=s_o, s_Sf=s_Sf, s_gtd=s_gtd, s_gTt=s_gTt, s_gt=s_gt)
    p_S = en(nc.psum_tensor("p_S", [128, 1024], F32))
    p_A = en(nc.psum_tensor("p_A", [128, 1024], F32))
    p_B = en(nc.psum_tensor("p_B", [128, 512], F32))
    p_C = en(nc.psum_tensor("p_C", [128, 256], F32))
    p_D = en(nc.psum_tensor("p_D", [128, 1024], F32))
    ts.update(p_S=p_S, p_A=p_A, p_B=p_B, p_C=p_C, p_D=p_D)

    with nc.Block() as block:
        @block.sync
        def _(eng):
            emit_all("sync", eng, ts)

        @block.scalar
        def _(eng):
            emit_all("scalar", eng, ts)

        @block.vector
        def _(eng):
            emit_all("vector", eng, ts)

        @block.tensor
        def _(eng):
            emit_all("tensor", eng, ts)

        @block.gpsimd
        def _(eng):
            emit_all("gpsimd", eng, ts)

    st_outer.close()
    return nc



class Emit:
    def __init__(self, which, eng, ts):
        self.w = which
        self.e = eng
        self.ts = ts
        self.sem = ts["sem"]
        self.C = {n: 0 for n in SEM_NAMES}

    def op(self, eng_name, fn, inc=None, incby=1):
        i = fn(self.e) if self.w == eng_name else None
        if inc is not None:
            self.C[inc] += incby
            if i is not None:
                i.then_inc(self.sem[inc], incby)
        return i

    def dma(self, queue, out, in_):
        eng_name = "sync" if queue == "qs" else "scalar"
        if self.w == eng_name:
            self.e.dma_start(out=out, in_=in_).then_inc(self.sem[queue], 16)
        self.C[queue] += 16

    def dmaT(self, out, in_):
        if self.w == "scalar":
            self.e.dma_start_transpose(out=out, in_=in_).then_inc(self.sem["qt"], 16)
        self.C["qt"] += 16

    def wait(self, eng_name, sems):
        for s in sems:
            if self.w == eng_name:
                self.e.wait_ge(self.sem[s], self.C[s])
            elif s == "qs" and self.w == "sync" and eng_name != "sync":
                self.e.wait_ge(self.sem[s], self.C[s])
            elif s == "qt" and self.w == "scalar" and eng_name != "scalar":
                self.e.wait_ge(self.sem[s], self.C[s])

    def drain(self, eng_name):
        if self.w == eng_name:
            self.e.drain()

    def barrier(self):
        for s in SEM_NAMES:
            self.e.wait_ge(self.sem[s], self.C[s])


def emit_all(which, eng, ts):
    em = Emit(which, eng, ts)
    phase_p0(em, ts)
    em.barrier()
    phase_f(em, ts)
    em.barrier()
    phase_a(em, ts)
    em.wait("sync", ("qs",))
    em.wait("scalar", ("qt",))



def q3v(ap, nb):
    return ap.rearrange("p (b k) -> p b k", k=16)


def bcast16(ap, nb):
    return ap.rearrange("p (b o) -> p b o", o=1).broadcast_to((128, nb, 16))


def emit_quant(em, ts, src_ap, dst_ap, ncols, gcol):
    v, a = "vector", "scalar"
    gs = ts["s_gs"]
    nb = ncols // 16
    q1, q2, q3, q4, sg = (ts["s_q1"][:, 0:ncols], ts["s_q2"][:, 0:ncols],
                          ts["s_q3"][:, 0:ncols], ts["s_q4"][:, 0:ncols],
                          ts["s_qsg"][:, 0:ncols])
    am, s8, dq, rd = (ts["s_qam"][:, 0:nb], ts["s_qs8"][:, 0:nb],
                      ts["s_qdq"][:, 0:nb], ts["s_qrd"][:, 0:nb])
    zt = ts["s_q2"][:, 0:nb]
    pt = ts["s_q2"][:, nb:2 * nb]
    ct = ts["s_q2"][:, 2 * nb:3 * nb]
    vt = ts["s_q3"][:, 0:nb]
    ut = ts["s_q3"][:, nb:2 * nb]
    em.op(v, lambda e: e.tensor_reduce(out=am, in_=q3v(src_ap, nb), axis=AX.X,
                                       op=ALU.max, apply_absolute_value=True))
    em.drain(v)
    em.op(v, lambda e: e.tensor_scalar(out=zt, in0=am, scalar1=gs[:, gcol + 1:gcol + 2],
                                       scalar2=224.0, op0=ALU.mult, op1=ALU.min))
    em.drain(v)
    em.op(v, lambda e: e.memset(pt, 2.0 ** -7))
    em.drain(v)
    for k in (8, 4, 2, 1):
        em.op(v, lambda e, k=k: e.tensor_scalar(out=ct, in0=pt, scalar1=float(2 ** k),
                                                scalar2=None, op0=ALU.mult))
        em.drain(v)
        em.op(v, lambda e: e.tensor_tensor(out=ct, in0=ct, in1=zt, op=ALU.is_le))
        em.drain(v)
        em.op(v, lambda e, k=k: e.tensor_scalar(out=ct, in0=ct,
                                                scalar1=float(2 ** k - 1),
                                                scalar2=1.0, op0=ALU.mult, op1=ALU.add))
        em.drain(v)
        em.op(v, lambda e: e.tensor_tensor(out=pt, in0=pt, in1=ct, op=ALU.mult))
        em.drain(v)
    em.op(v, lambda e: e.reciprocal(out=ct, in_=pt))
    em.drain(v)
    em.op(v, lambda e: e.scalar_tensor_tensor(out=ut, in0=zt, scalar=8.0, in1=ct,
                                              op0=ALU.mult, op1=ALU.mult))
    em.drain(v)
    em.op(v, lambda e: e.memset(vt, 0.0))
    em.drain(v)
    for k in (16, 8, 4, 2, 1):
        em.op(v, lambda e, k=k: e.scalar_tensor_tensor(out=ct, in0=vt,
                                                       scalar=float(k) - 0.5, in1=ut,
                                                       op0=ALU.add, op1=ALU.is_le))
        em.drain(v)
        em.op(v, lambda e, k=k: e.scalar_tensor_tensor(out=vt, in0=ct,
                                                       scalar=float(k), in1=vt,
                                                       op0=ALU.mult, op1=ALU.add))
        em.drain(v)
    em.op(v, lambda e: e.scalar_tensor_tensor(out=s8, in0=vt, scalar=0.125, in1=pt,
                                              op0=ALU.mult, op1=ALU.mult))
    em.drain(v)
    em.op(v, lambda e: e.tensor_scalar(out=dq, in0=s8, scalar1=gs[:, gcol + 2:gcol + 3],
                                       scalar2=None, op0=ALU.mult))
    em.op(v, lambda e: e.tensor_scalar(out=rd, in0=s8, scalar1=1e-30,
                                       scalar2=None, op0=ALU.max))
    em.drain(v)
    em.op(v, lambda e: e.reciprocal(out=rd, in_=rd))
    em.drain(v)
    em.op(v, lambda e: e.tensor_scalar(out=rd, in0=rd, scalar1=gs[:, gcol:gcol + 1],
                                       scalar2=None, op0=ALU.mult))
    em.drain(v)
    em.op(v, lambda e: e.tensor_tensor(out=q3v(q1, nb), in0=q3v(src_ap, nb),
                                       in1=bcast16(rd, nb), op=ALU.mult),
          inc="vv")
    em.wait(a, ("vv",))
    em.op(a, lambda e: e.activation(out=sg, in_=q1, func=AF.Sign))
    em.op(a, lambda e: e.activation(out=q2, in_=q1, func=AF.Abs), inc="aa")
    em.wait(v, ("aa",))
    em.op(v, lambda e: e.tensor_scalar(out=q3, in0=q2, scalar1=4.0, scalar2=2.0,
                                       op0=ALU.is_ge, op1=ALU.subtract))
    em.op(v, lambda e: e.tensor_scalar(out=q4, in0=q2, scalar1=8.0, scalar2=2.0,
                                       op0=ALU.is_ge, op1=ALU.subtract))
    em.drain(v)
    em.op(v, lambda e: e.tensor_tensor(out=q3, in0=q3, in1=q4, op=ALU.mult))
    em.drain(v)
    em.op(v, lambda e: e.scalar_tensor_tensor(out=q3, in0=q2, scalar=12.0, in1=q3,
                                              op0=ALU.min, op1=ALU.mult))
    em.drain(v)
    em.op(v, lambda e: e.tensor_scalar(out=q4, in0=q3, scalar1=2.0, scalar2=None,
                                       op0=ALU.is_ge))
    em.drain(v)
    for th in (6.0, 10.0, 14.0):
        em.op(v, lambda e, th=th: e.scalar_tensor_tensor(out=q4, in0=q3, scalar=th,
                                                         in1=q4, op0=ALU.is_ge,
                                                         op1=ALU.add))
        em.drain(v)
    em.op(v, lambda e: e.tensor_scalar(out=q3, in0=q2, scalar1=4.0, scalar2=1.0,
                                       op0=ALU.is_ge, op1=ALU.add))
    em.op(v, lambda e: e.tensor_scalar(out=q2, in0=q2, scalar1=8.0, scalar2=1.0,
                                       op0=ALU.is_ge, op1=ALU.add))
    em.drain(v)
    em.op(v, lambda e: e.tensor_tensor(out=q3, in0=q3, in1=q2, op=ALU.mult))
    em.drain(v)
    em.op(v, lambda e: e.tensor_tensor(out=q4, in0=q4, in1=q3, op=ALU.mult))
    em.drain(v)
    em.op(v, lambda e: e.tensor_tensor(out=q3v(q4, nb), in0=q3v(q4, nb),
                                       in1=bcast16(dq, nb), op=ALU.mult))
    em.drain(v)
    em.op(v, lambda e: e.tensor_tensor(out=dst_ap, in0=q4, in1=sg, op=ALU.mult),
          inc="vv")
    em.drain(v)


def emit_ln(em, ts, x_ap, xn_ap):
    v, a = "vector", "scalar"
    ln = ts["s_ln"]
    em.op(v, lambda e: e.bn_stats(out=ln[:, 0:6], in_=x_ap[:, 0:512]))
    em.op(v, lambda e: e.bn_stats(out=ln[:, 6:12], in_=x_ap[:, 512:1024]))
    em.drain(v)
    em.op(v, lambda e: e.bn_aggr(out=ln[:, 12:14], in_=ln[:, 0:12]))
    em.drain(v)
    em.op(v, lambda e: e.tensor_scalar(out=ln[:, 14:15], in0=ln[:, 13:14],
                                       scalar1=1e-5, scalar2=None, op0=ALU.add),
          inc="vv")
    em.drain(v)
    em.wait("gpsimd", ("vv",))
    em.op("gpsimd", lambda e: e.tensor_tensor(
        out=ln[:, 15:16], in0=ln[:, 14:15], in1=ts["s_mh"][:], op=ALU.pow),
        inc="gg")
    em.wait(v, ("gg",))
    em.op(v, lambda e: e.tensor_scalar(out=xn_ap, in0=x_ap, scalar1=ln[:, 12:13],
                                       scalar2=None, op0=ALU.subtract))
    em.drain(v)
    em.op(v, lambda e: e.tensor_scalar(out=xn_ap, in0=xn_ap, scalar1=ln[:, 15:16],
                                       scalar2=None, op0=ALU.mult), inc="vv")
    em.drain(v)



def phase_p0(em, ts):
    v, a, t, s, g = "vector", "scalar", "tensor", "sync", "gpsimd"
    dram = ts["dram"]
    nc_gs = ts["s_gs"]
    em.op(v, lambda e: e.memset(ts["s_mh"][:], -0.5))
    em.dma("qs", ts["s_msk"][:], dram["masks"][:, :])
    em.dma("qs", ts["s_nw"][:], dram["norm_w_rep"][:, :])
    em.dma("qs", nc_gs[:, 0:1], dram["s_in_rep"][:, :])
    em.dma("qs", nc_gs[:, 3:4], dram["s_in_down_rep"][:, :])
    cw3 = ts["s_cw"][:].rearrange("p (c i) -> p c i", i=4)
    em.dma("qs", cw3, dram["conv_w"][:, :].rearrange("(c p) i -> p c i", p=128))
    em.wait(v, ("qs",))
    for c in (0, 3):
        em.op(v, lambda e, c=c: e.tensor_scalar(out=nc_gs[:, c + 1:c + 2],
                                                in0=nc_gs[:, c:c + 1],
                                                scalar1=1.0 / 12.0, scalar2=None,
                                                op0=ALU.mult))
        em.op(v, lambda e, c=c: e.tensor_copy(out=nc_gs[:, c + 2:c + 3],
                                              in_=nc_gs[:, c:c + 1]))
        em.drain(v)
        em.op(v, lambda e, c=c: e.reciprocal(out=nc_gs[:, c + 2:c + 3],
                                             in_=nc_gs[:, c + 2:c + 3]))
        em.drain(v)

    em.op(v, lambda e: e.memset(ts["s_q2"][:, 0:4], 0.0), inc="vv")
    em.drain(v)
    for wi, (wname, nch, ncols) in enumerate([("gw_shard", 4, 1024),
                                              ("uw_shard", 4, 1024),
                                              ("dw_shard", 1, 4096)]):
        whbm = dram[wname]
        em.op(v, lambda e: e.memset(nc_gs[:, 15:16], 0.0), inc="vv")
        nparts = nch * (ncols // 1024)
        for pp in range(nparts):
            buf = ts["s_wch"] if pp % 2 == 0 else ts["s_wch2"]
            em.wait(s, ("vv",))
            if ncols == 1024:
                em.dma("qs", buf[:, 0:1024], whbm[pp * 128:(pp + 1) * 128, :])
            else:
                em.dma("qs", buf[:, 0:1024], whbm[0:128, pp * 1024:(pp + 1) * 1024])
            em.wait(v, ("qs",))
            em.op(v, lambda e, buf=buf: e.tensor_reduce(
                out=ts["s_qam"][:, 0:1], in_=buf[:, 0:1024], axis=AX.X,
                op=ALU.max, apply_absolute_value=True))
            em.drain(v)
            em.op(v, lambda e: e.tensor_tensor(
                out=nc_gs[:, 15:16], in0=nc_gs[:, 15:16],
                in1=ts["s_qam"][:, 0:1], op=ALU.max), inc="vv")
            em.drain(v)
        em.wait(s, ("vv",))
        em.dma("qs", dram["wmax_scratch"][:].rearrange("(p o) -> p o", o=1),
               nc_gs[:, 15:16])
        em.wait(s, ("qs",))
        em.dma("qs", ts["s_q3"][0:1, 0:128],
               dram["wmax_scratch"][:].rearrange("(o f) -> o f", o=1))
        em.wait(v, ("qs",))
        em.op(v, lambda e: e.tensor_reduce(out=ts["s_q3"][0:1, 128:129],
                                           in_=ts["s_q3"][0:1, 0:128], axis=AX.X,
                                           op=ALU.max, apply_absolute_value=True))
        em.drain(v)
        em.op(v, lambda e: e.tensor_copy(
            out=ts["s_q3"][0:1, 0:128],
            in_=ts["s_q3"][0:1, 128:129].broadcast_to((1, 128))), inc="vv")
        em.drain(v)
        em.wait(s, ("vv",))
        em.dma("qs", dram["wmax_scratch"][:].rearrange("(o f) -> o f", o=1),
               ts["s_q3"][0:1, 0:128])
        em.wait(s, ("qs",))
        em.dma("qs", ts["s_q2"][:, wi:wi + 1],
               dram["wmax_scratch"][:].rearrange("(p o) -> p o", o=1))
    em.wait(s, ("qs",))
    em.dma("qs", dram["cc_max_in"][:, :], ts["s_q2"][:, 0:4])
    em.wait(g, ("qs",))
    em.op(g, lambda e: e.collective_compute(
        "AllReduce", ALU.max, replica_groups=[list(range(8))],
        ins=[dram["cc_max_in"][:, :]], outs=[dram["cc_max_out"][:, :]]),
        inc="gg")
    em.wait(s, ("gg",))
    em.dma("qs", ts["s_q2"][:, 4:8], dram["cc_max_out"][:, :])
    em.wait(v, ("qs",))
    for wi in range(3):
        gcol = 6 + 3 * wi
        em.op(v, lambda e, wi=wi, gcol=gcol: e.reciprocal(
            out=nc_gs[:, gcol:gcol + 1], in_=ts["s_q2"][:, 4 + wi:5 + wi]))
        em.drain(v)
        em.op(v, lambda e, gcol=gcol: e.tensor_scalar(
            out=nc_gs[:, gcol:gcol + 1], in0=nc_gs[:, gcol:gcol + 1],
            scalar1=2688.0, scalar2=None, op0=ALU.mult))
        em.drain(v)
        em.op(v, lambda e, gcol=gcol: e.tensor_scalar(
            out=nc_gs[:, gcol + 1:gcol + 2], in0=nc_gs[:, gcol:gcol + 1],
            scalar1=1.0 / 12.0, scalar2=None, op0=ALU.mult))
        em.op(v, lambda e, gcol=gcol: e.tensor_copy(
            out=nc_gs[:, gcol + 2:gcol + 3], in_=nc_gs[:, gcol:gcol + 1]))
        em.drain(v)
        em.op(v, lambda e, gcol=gcol: e.reciprocal(
            out=nc_gs[:, gcol + 2:gcol + 3], in_=nc_gs[:, gcol + 2:gcol + 3]),
            inc="vv")
        em.drain(v)
    for wi, (wname, nch, ncols, cci, cco) in enumerate(
            [("gw_shard", 4, 1024, "cc_g_in", "cc_g_out"),
             ("uw_shard", 4, 1024, "cc_u_in", "cc_u_out"),
             ("dw_shard", 1, 4096, "cc_d_in", "cc_d_out")]):
        whbm = dram[wname]
        gcol = 6 + 3 * wi
        em.wait(v, ("qt",))
        nparts = nch * (ncols // 1024)
        for pp in range(nparts):
            buf = ts["s_wch"] if pp % 2 == 0 else ts["s_wch2"]
            em.wait(s, ("vv",))
            if ncols == 1024:
                em.dma("qs", buf[:, 0:1024], whbm[pp * 128:(pp + 1) * 128, :])
            else:
                em.dma("qs", buf[:, 0:1024], whbm[0:128, pp * 1024:(pp + 1) * 1024])
            em.wait(v, ("qs",))
            dst = ts["s_wnat"][:, pp * 1024:(pp + 1) * 1024]
            emit_quant(em, ts, buf[:, 0:1024], dst, 1024, gcol)
        em.wait("scalar", ("vv",))
        if ncols == 1024:
            wt3 = ts["s_wT"][:, 0:4096].rearrange("p (hc ch r) -> p hc ch r",
                                                  hc=8, ch=4, r=128)
            for ch in range(4):
                em.dmaT(wt3[:, :, ch, :], ts["s_wnat"][:, ch * 1024:(ch + 1) * 1024])
        else:
            wt3 = ts["s_wT"][:, 0:4096].rearrange("p (ic r) -> p ic r", r=128)
            em.dmaT(wt3, ts["s_wnat"][:, 0:4096])
        em.wait("sync", ("qt",))
        if ncols == 1024:
            em.dma("qs", dram[cci][:, :].rearrange("(hc pp) n -> pp hc n", pp=128),
                   ts["s_wT"][:, 0:4096].rearrange("p (hc n) -> p hc n", n=512))
        else:
            em.dma("qs", dram[cci][:, :].rearrange("(ic pp) n -> pp ic n", pp=128),
                   ts["s_wT"][:, 0:4096].rearrange("p (ic n) -> p ic n", n=128))
        em.wait(g, ("qs",))
        em.op(g, lambda e, cci=cci, cco=cco: e.collective_compute(
            "AllGather", ALU.bypass, replica_groups=[list(range(8))],
            ins=[dram[cci][:, :]], outs=[dram[cco][:, :, :]]), inc="gg")



def phase_f(em, ts):
    v, a, t, s = "vector", "scalar", "tensor", "sync"
    dram = ts["dram"]
    XO = ts["XO"]
    for tt in range(NTO):
        em.wait(s, ("vv",))
        em.dma("qs", ts["s_xt"][:], XO[tt * 128:(tt + 1) * 128, :])
        em.wait(v, ("qs",))
        emit_ln(em, ts, ts["s_xt"][:], ts["s_xn"][:])
        em.wait(v, ("qt",))
        emit_quant(em, ts, ts["s_xn"][:], ts["s_xqb"][:, 0:1024], 1024, 0)
        em.wait("scalar", ("vv",))
        xt3 = ts["s_xqT"][:].rearrange("p (hch t) -> p hch t", hch=8)[
            :, :, tt * 128: tt * 128 + 128]
        em.dmaT(xt3, ts["s_xqb"][:, 0:1024])
    em.wait(t, ("qt",))
    em.wait(v, ("qt",))
    em.wait(s, ("gg",))
    for oc in range(4):
        em.wait(s, ("tt",))
        for blk2 in range(2):
            r = 2 * oc + blk2
            em.dma("qs",
                   ts["s_wgc"][:].rearrange("p (c n) -> p c n", n=1024)[
                       :, :, blk2 * 512:(blk2 + 1) * 512],
                   dram["cc_g_out"][r, :, :].rearrange("(c p) n -> p c n", p=128))
            em.dma("qs",
                   ts["s_wuc"][:].rearrange("p (c n) -> p c n", n=1024)[
                       :, :, blk2 * 512:(blk2 + 1) * 512],
                   dram["cc_u_out"][r, :, :].rearrange("(c p) n -> p c n", p=128))
        em.wait(t, ("qs",))
        for tt in range(NTO):
            em.wait(t, ("vv", "aa"))
            for ch in range(8):
                for hf in range(2):
                    em.op(t, lambda e, ch=ch, tt=tt, hf=hf: e.matmul(
                        out=ts["p_A"][:, hf * 512:(hf + 1) * 512],
                        lhsT=ts["s_xqT"][:, ch * 2048 + tt * 128: ch * 2048 + (tt + 1) * 128],
                        rhs=ts["s_wgc"][:, ch * 1024 + hf * 512: ch * 1024 + (hf + 1) * 512],
                        start=(ch == 0), stop=(ch == 7)))
            for ch in range(8):
                for hf in range(2):
                    em.op(t, lambda e, ch=ch, tt=tt, hf=hf: e.matmul(
                        out=ts["p_D"][:, hf * 512:(hf + 1) * 512],
                        lhsT=ts["s_xqT"][:, ch * 2048 + tt * 128: ch * 2048 + (tt + 1) * 128],
                        rhs=ts["s_wuc"][:, ch * 1024 + hf * 512: ch * 1024 + (hf + 1) * 512],
                        start=(ch == 0), stop=(ch == 7)),
                        inc="tt" if (ch == 7 and hf == 1) else None)
            em.wait(a, ("tt",))
            em.op(a, lambda e: e.activation(out=ts["s_sg"][:], in_=ts["p_A"][:],
                                            func=AF.Silu), inc="aa")
            em.wait(v, ("aa",))
            em.op(v, lambda e: e.tensor_tensor(out=ts["s_hid"][:], in0=ts["s_sg"][:],
                                               in1=ts["p_D"][:], op=ALU.mult), inc="vv")
            em.drain(v)
            em.wait(v, ("qs",))
            emit_quant(em, ts, ts["s_hid"][:], ts["s_hqb"][:], 1024, 3)
            em.wait(s, ("vv",))
            em.dma("qs", dram["hq_scratch"][tt * 128:(tt + 1) * 128,
                                            oc * 1024:(oc + 1) * 1024],
                   ts["s_hqb"][:])
    em.wait(s, ("gg", "qt"))
    for r in range(8):
        em.dma("qs",
               ts["s_wT"][:].rearrange("p (ic n) -> p ic n", n=1024)[
                   :, :, r * 128:(r + 1) * 128],
               dram["cc_d_out"][r, :, :].rearrange("(ic p) n -> p ic n", p=128))
    em.wait(t, ("qs",))
    for tb in range(4):
        em.wait("scalar", ("qs",))
        outT = ts["s_hqT"][:].rearrange("p (ic r) -> p ic r", r=512)
        em.dmaT(outT, dram["hq_scratch"][tb * 512:(tb + 1) * 512, :])
        em.wait(t, ("qt",))
        for tsub in range(4):
            tt = tb * 4 + tsub
            em.wait(t, ("aa",))
            for ic in range(32):
                for half in range(2):
                    em.op(t, lambda e, ic=ic, half=half, tsub=tsub: e.matmul(
                        out=ts["p_A"][:, half * 512:(half + 1) * 512],
                        lhsT=ts["s_hqT"][:, ic * 512 + tsub * 128: ic * 512 + (tsub + 1) * 128],
                        rhs=ts["s_wT"][:, ic * 1024 + half * 512: ic * 1024 + (half + 1) * 512],
                        start=(ic == 0), stop=(ic == 31)),
                        inc="tt" if (ic == 31 and half == 1) else None)
            em.wait(a, ("tt", "qs"))
            em.op(a, lambda e: e.activation(out=ts["s_gf"][:], in_=ts["p_A"][:],
                                            func=AF.Silu), inc="aa")
            em.wait(s, ("aa",))
            em.dma("qs", dram["g_scratch"][tt * 128:(tt + 1) * 128, :], ts["s_gf"][:])
            em.dma("qs", dram["g_scratch"][HALF + tt * 128:HALF + (tt + 1) * 128, :],
                   ts["s_gf"][:])



def phase_a(em, ts):
    v, a, t, s = "vector", "scalar", "tensor", "sync"
    dram = ts["dram"]
    X = ts["X"]
    msk = ts["s_msk"]
    trilS = msk[:, 0:64]
    triuI = msk[:, 64:128]
    id4 = msk[:, 128:384]
    LP = L + 16

    for wname, sbname, dout in [("Wq", "s_Wq", 512), ("Wk", "s_Wk", 512),
                                ("Wv", "s_Wv", 1024), ("Wb", "s_Wb", 4),
                                ("Wg", "s_Wg", 1024), ("Wo", "s_Wo", 1024)]:
        for ch in range(8):
            em.wait(s, ("vv",))
            em.dma("qs", ts["s_xt"][:, 0:dout],
                   dram[wname][ch * 128:(ch + 1) * 128, :])
            em.wait(v, ("qs",))
            em.op(v, lambda e, sbname=sbname, ch=ch, dout=dout: e.tensor_copy(
                out=ts[sbname][:, ch * dout:(ch + 1) * dout],
                in_=ts["s_xt"][:, 0:dout]), inc="vv")
            em.drain(v)

    em.op(v, lambda e: e.memset(ts["s_Sbf"][:], 0.0))
    em.op(v, lambda e: e.memset(ts["s_Sf"][:], 0.0))
    em.op(v, lambda e: e.memset(ts["s_mh"][:], -0.5), inc="vv")
    em.wait("gpsimd", ("vv",))
    for ch in range(8):
        em.op(v, lambda e, ch=ch: e.memset(ts["s_xnT"][:, ch * LP:ch * LP + 16], 0.0))
    em.op(v, lambda e: e.memset(ts["s_uf"][:], 0.0), inc="vv")

    for tt in range(NT):
        em.wait(s, ("vv",))
        em.dma("qs", ts["s_xt"][:], X[tt * 128:(tt + 1) * 128, :])
        em.wait(v, ("qs",))
        emit_ln(em, ts, ts["s_xt"][:], ts["s_xn"][:])
        em.wait(v, ("qt",))
        em.op(v, lambda e: e.tensor_copy(
            out=ts["s_xnb"][:, 0:1024], in_=ts["s_xn"][:]), inc="vv")
        em.drain(v)
        em.wait("scalar", ("vv",))
        nt3 = ts["s_xnT"][:].rearrange("p (dch t) -> p dch t", dch=8)[
            :, :, 16 + tt * 128: 16 + tt * 128 + 128]
        em.dmaT(nt3, ts["s_xnb"][:, 0:1024])
    em.wait(v, ("qt",))
    em.wait(t, ("qt",))

    for tt in range(NT):
        cw3 = ts["s_cw"][:].rearrange("p (c i) -> p c i", i=4)
        base = 13 + tt * 128
        def xv(i, base=base):
            return ts["s_xnT"][:].rearrange("p (c t) -> p c t", c=8)[
                :, :, base + i: base + i + 128]
        q13 = ts["s_q1"][:].rearrange("p (c t) -> p c t", t=128)
        q23 = ts["s_q2"][:].rearrange("p (c t) -> p c t", t=128)
        em.wait(v, ("aa", "qs"))
        em.op(v, lambda e: e.tensor_tensor(
            out=q13, in0=xv(0), in1=cw3[:, :, 0:1].broadcast_to((128, 8, 128)),
            op=ALU.mult))
        for i in (1, 2, 3):
            em.op(v, lambda e, i=i: e.tensor_tensor(
                out=q23, in0=xv(i), in1=cw3[:, :, i:i + 1].broadcast_to((128, 8, 128)),
                op=ALU.mult))
            em.drain(v)
            em.op(v, lambda e: e.tensor_tensor(out=ts["s_q1"][:], in0=ts["s_q1"][:],
                                               in1=ts["s_q2"][:], op=ALU.add))
            em.drain(v)
        em.drain(v)
        em.op(v, lambda e: e.nop(), inc="vv")
        em.wait(a, ("vv",))
        em.op(a, lambda e: e.activation(out=ts["s_hT"][:], in_=ts["s_q1"][:],
                                        func=AF.Silu), inc="aa")
        em.wait(t, ("aa", "vv"))

        for ch in range(8):
            hsl = ts["s_hT"][:, ch * 128:(ch + 1) * 128]
            em.op(t, lambda e, ch=ch, hsl=hsl: e.matmul(
                out=ts["p_A"][:, 0:512], lhsT=hsl,
                rhs=ts["s_Wq"][:, ch * 512:(ch + 1) * 512],
                start=(ch == 0), stop=(ch == 7)))
            em.op(t, lambda e, ch=ch, hsl=hsl: e.matmul(
                out=ts["p_A"][:, 512:1024], lhsT=hsl,
                rhs=ts["s_Wk"][:, ch * 512:(ch + 1) * 512],
                start=(ch == 0), stop=(ch == 7)))
            em.op(t, lambda e, ch=ch, hsl=hsl: e.matmul(
                out=ts["p_D"][:, 0:4], lhsT=hsl,
                rhs=ts["s_Wb"][:, ch * 4:(ch + 1) * 4],
                start=(ch == 0), stop=(ch == 7)), inc="tt" if ch == 7 else None)
        em.wait(a, ("tt",))
        st = ts["s_st"]
        em.wait(v, ("tt", "qt"))
        em.op(v, lambda e: e.tensor_copy(out=ts["s_qk"][:], in_=ts["p_A"][:]),
              inc="vv")
        em.drain(v)
        em.wait(a, ("vv",))
        for h in range(8):
            em.op(a, lambda e, h=h: e.activation(
                out=ts["s_q3"][:, h * 128:(h + 1) * 128],
                in_=ts["s_qk"][:, h * 128:(h + 1) * 128],
                func=AF.Square, accum_out=st[:, 4 + h if h < 4 else h - 4:
                                             5 + h if h < 4 else h - 3]))
        em.op(a, lambda e: e.activation(out=ts["s_beta"][:], in_=ts["p_D"][:, 0:4],
                                        func=AF.Sigmoid), inc="aa")
        em.wait("scalar", ("vv", "tt"))
        em.dmaT(ts["s_qkT"][:].rearrange("p (c r) -> p c r", r=128), ts["s_qk"][:])
        em.wait("gpsimd", ("aa",))
        em.op("gpsimd", lambda e: e.tensor_scalar(out=st[:, 8:16], in0=st[:, 0:8],
                                                  scalar1=1e-24, scalar2=None,
                                                  op0=ALU.max))
        em.drain("gpsimd")
        em.op("gpsimd", lambda e: e.tensor_tensor(
            out=st[:, 8:16], in0=st[:, 8:16],
            in1=ts["s_mh"][:].broadcast_to((128, 8)), op=ALU.pow))
        em.drain("gpsimd")
        em.op("gpsimd", lambda e: e.tensor_scalar(out=st[:, 8:16], in0=st[:, 8:16],
                                                  scalar1=1e12, scalar2=None,
                                                  op0=ALU.min))
        em.drain("gpsimd")
        em.op("gpsimd", lambda e: e.tensor_tensor(out=st[:, 16:20], in0=st[:, 8:12],
                                                  in1=ts["s_beta"][:], op=ALU.mult))
        em.drain("gpsimd")
        em.op("gpsimd", lambda e: e.tensor_copy(out=st[:, 20:24], in_=st[:, 16:20]))
        em.drain("gpsimd")
        em.op("gpsimd", lambda e: e.tensor_tensor(out=st[:, 16:20], in0=st[:, 16:20],
                                                  in1=st[:, 8:12], op=ALU.mult),
              inc="gg")
        em.wait(v, ("gg",))
        em.wait(t, ("gg", "qt"))
        for ch in range(8):
            hsl = ts["s_hT"][:, ch * 128:(ch + 1) * 128]
            for half in range(2):
                em.op(t, lambda e, ch=ch, hsl=hsl, half=half: e.matmul(
                    out=ts["p_A"][:, half * 512:(half + 1) * 512], lhsT=hsl,
                    rhs=ts["s_Wv"][:, ch * 1024 + half * 512: ch * 1024 + (half + 1) * 512],
                    start=(ch == 0), stop=(ch == 7)),
                    inc="tt" if (ch == 7 and half == 1) else None)
        em.wait(v, ("tt",))
        em.op(v, lambda e: e.tensor_copy(out=ts["s_v"][:], in_=ts["p_A"][:]), inc="vv")
        em.wait(t, ("vv",))
        for ch in range(8):
            hsl = ts["s_hT"][:, ch * 128:(ch + 1) * 128]
            for half in range(2):
                em.op(t, lambda e, ch=ch, hsl=hsl, half=half: e.matmul(
                    out=ts["p_A"][:, half * 512:(half + 1) * 512], lhsT=hsl,
                    rhs=ts["s_Wg"][:, ch * 1024 + half * 512: ch * 1024 + (half + 1) * 512],
                    start=(ch == 0), stop=(ch == 7)),
                    inc="tt" if (ch == 7 and half == 1) else None)
        em.wait(v, ("tt",))
        em.op(v, lambda e: e.tensor_copy(out=ts["s_g"][:], in_=ts["p_A"][:]), inc="vv")
        em.drain(v)

        for c in range(2):
            hb = slice(64 * c, 64 * c + 64)
            tpK = 64 * c
            emit_chunk(em, ts, tt, c, hb, tpK, trilS, triuI, id4)

        st = ts["s_st"]
        em.wait(a, ("vv", "qs"))
        for h in range(4):
            em.op(a, lambda e, h=h: e.activation(
                out=ts["s_q2"][:, h * 256:(h + 1) * 256],
                in_=ts["s_o"][:, h * 256:(h + 1) * 256],
                func=AF.Square, accum_out=st[:, h:h + 1]))
        em.op(a, lambda e: e.activation(out=ts["s_q4"][:], in_=ts["s_g"][:],
                                        func=AF.Silu), inc="aa")
        em.wait("gpsimd", ("aa",))
        em.op("gpsimd", lambda e: e.tensor_scalar(out=st[:, 4:8], in0=st[:, 0:4],
                                                  scalar1=1.0 / 256.0, scalar2=1e-5,
                                                  op0=ALU.mult, op1=ALU.add))
        em.drain("gpsimd")
        em.op("gpsimd", lambda e: e.tensor_tensor(
            out=st[:, 4:8], in0=st[:, 4:8],
            in1=ts["s_mh"][:].broadcast_to((128, 4)), op=ALU.pow), inc="gg")
        em.wait(v, ("gg", "aa"))
        o4 = ts["s_o"][:].rearrange("p (h f) -> p h f", f=256)
        rs4 = st[:, 4:8].rearrange("p (h o) -> p h o", o=1).broadcast_to((128, 4, 256))
        em.op(v, lambda e: e.tensor_tensor(out=o4, in0=o4, in1=rs4, op=ALU.mult))
        em.drain(v)
        nw4 = ts["s_nw"][:].rearrange("p (o f) -> p o f", o=1).broadcast_to((128, 4, 256))
        em.op(v, lambda e: e.tensor_tensor(out=o4, in0=o4, in1=nw4, op=ALU.mult))
        em.drain(v)
        em.wait(v, ("qt",))
        em.op(v, lambda e: e.tensor_tensor(out=ts["s_gtd"][:], in0=ts["s_o"][:],
                                           in1=ts["s_q4"][:], op=ALU.mult), inc="vv")
        em.drain(v)
        em.wait("scalar", ("vv",))
        outT = ts["s_gTt"][:].rearrange("p (c r) -> p c r", r=128)
        em.dmaT(outT, ts["s_gtd"][:])
        em.wait(t, ("qt", "aa", "vv"))
        for ch in range(8):
            for half in range(2):
                em.op(t, lambda e, ch=ch, half=half: e.matmul(
                    out=ts["p_A"][:, half * 512:(half + 1) * 512],
                    lhsT=ts["s_gTt"][:, ch * 128:(ch + 1) * 128],
                    rhs=ts["s_Wo"][:, ch * 1024 + half * 512: ch * 1024 + (half + 1) * 512],
                    start=(ch == 0), stop=(ch == 7)),
                    inc="tt" if (ch == 7 and half == 1) else None)
        em.wait(s, ("vv",))
        em.dma("qs", ts["s_gt"][:], dram["g_scratch"][tt * 128:(tt + 1) * 128, :])
        em.dma("qs", ts["s_xt"][:], X[tt * 128:(tt + 1) * 128, :])
        em.wait(v, ("tt", "qs"))
        em.op(v, lambda e: e.tensor_tensor(out=ts["s_q1"][:], in0=ts["p_A"][:],
                                           in1=ts["s_gt"][:], op=ALU.mult))
        em.drain(v)
        em.op(v, lambda e: e.tensor_tensor(out=ts["s_q2"][:], in0=ts["s_q1"][:],
                                           in1=ts["s_xt"][:], op=ALU.add), inc="vv")
        em.drain(v)
        em.wait(s, ("vv",))
        em.dma("qs", ts["OUT"][tt * 128:(tt + 1) * 128, :], ts["s_q2"][:])


def emit_chunk(em, ts, tt, c, hb, tpK, trilS, triuI, id4):
    v, a, t = "vector", "scalar", "tensor"
    st = ts["s_st"]
    tok0 = 64 * c
    tpO = (0, 64 * c)
    tpKK = (64 * c, 64 * c)

    em.wait(t, ("vv", "aa"))
    for h in range(4):
        kTc = ts["s_qkT"][:, 512 + h * 128 + tok0: 512 + h * 128 + tok0 + 64]
        em.op(t, lambda e, h=h, kTc=kTc: e.matmul(
            out=ts["p_C"][hb, h * 64:(h + 1) * 64], lhsT=kTc, rhs=kTc,
            start=True, stop=True, tile_position=tpO),
            inc="tt" if h == 3 else None)
    em.wait(v, ("tt",))
    g3 = ts["p_C"][hb, :].rearrange("p (h f) -> p h f", f=64)
    tril3 = trilS[hb, :].rearrange("p (o f) -> p o f", o=1).broadcast_to((64, 4, 64))
    em.op(v, lambda e: e.tensor_tensor(out=ts["s_C"][hb, :].rearrange(
        "p (h f) -> p h f", f=64), in0=g3, in1=tril3, op=ALU.mult))
    em.drain(v)
    s3 = st[hb, 16:20].rearrange("p (h o) -> p h o", o=1).broadcast_to((64, 4, 64))
    em.op(v, lambda e: e.tensor_tensor(out=ts["s_dg"][hb, :].rearrange(
        "p (h f) -> p h f", f=64), in0=id4[hb, :].rearrange("p (h f) -> p h f", f=64),
        in1=s3, op=ALU.mult))
    em.op(v, lambda e: e.tensor_tensor(out=ts["s_At"][hb, :].rearrange(
        "p (h f) -> p h f", f=64), in0=ts["s_C"][hb, :].rearrange(
        "p (h f) -> p h f", f=64), in1=s3, op=ALU.mult), inc="vv")
    em.drain(v)
    em.wait(t, ("vv",))
    for h in range(4):
        em.op(t, lambda e, h=h: e.matmul(
            out=ts["p_B"][hb, h * 64:(h + 1) * 64],
            lhsT=ts["s_C"][hb, h * 64:(h + 1) * 64],
            rhs=ts["s_dg"][hb, h * 64:(h + 1) * 64],
            start=True, stop=True, tile_position=tpKK),
            inc="tt" if h == 3 else None)
    em.wait(v, ("tt",))
    em.op(v, lambda e: e.tensor_copy(out=ts["s_AtT"][hb, :], in_=ts["p_B"][hb, 0:256]),
          inc="vv")
    em.drain(v)
    em.wait(t, ("vv",))
    for h in range(4):
        kTc = ts["s_qkT"][:, 512 + h * 128 + tok0: 512 + h * 128 + tok0 + 64]
        em.op(t, lambda e, h=h, kTc=kTc: e.matmul(
            out=ts["p_A"][hb, h * 256:(h + 1) * 256], lhsT=kTc,
            rhs=ts["s_Sbf"][:, h * 256:(h + 1) * 256],
            start=True, stop=True, tile_position=tpO),
            inc="tt" if h == 3 else None)
    def pwY(Lv):
        return ts["s_pw"][hb, Lv * 512: Lv * 512 + 256]
    def pwYT(Lv):
        return ts["s_pw"][hb, Lv * 512 + 256: Lv * 512 + 512]
    prevY, prevYT = ts["s_At"], ts["s_AtT"]
    for Lv in range(4):
        em.wait(t, ("vv",))
        for h in range(4):
            hs = slice(h * 64, (h + 1) * 64)
            em.op(t, lambda e, h=h, hs=hs: e.matmul(
                out=ts["p_B"][hb, h * 64:(h + 1) * 64],
                lhsT=prevYT[hb, hs], rhs=prevY[hb, hs],
                start=True, stop=True, tile_position=tpKK))
            em.op(t, lambda e, h=h, hs=hs: e.matmul(
                out=ts["p_B"][hb, 256 + h * 64:256 + (h + 1) * 64],
                lhsT=prevY[hb, hs], rhs=prevYT[hb, hs],
                start=True, stop=True, tile_position=tpKK),
                inc="tt" if h == 3 else None)
        em.wait(v, ("tt",))
        if Lv < 3:
            em.op(v, lambda e, Lv=Lv: e.tensor_copy(
                out=ts["s_pw"][hb, Lv * 512:(Lv + 1) * 512], in_=ts["p_B"][hb, :]),
                inc="vv")
        else:
            em.op(v, lambda e, Lv=Lv: e.tensor_copy(
                out=ts["s_pw"][hb, Lv * 512:(Lv + 1) * 512], in_=ts["p_B"][hb, :]),
                inc="vv")
        em.drain(v)
        prevY, prevYT = None, None
        prevY = _PwSlice(ts, Lv, 0)
        prevYT = _PwSlice(ts, Lv, 1)
    em.wait(v, ("tt",))
    ks3 = ts["p_A"][hb, :].rearrange("p (h f) -> p h f", f=256)
    rk3 = st[hb, 8:12].rearrange("p (h o) -> p h o", o=1).broadcast_to((64, 4, 256))
    u3 = ts["s_uf"][hb, :].rearrange("p (h f) -> p h f", f=256)
    v3 = ts["s_v"][hb, :].rearrange("p (h f) -> p h f", f=256)
    em.op(v, lambda e: e.tensor_tensor(out=u3, in0=ks3, in1=rk3, op=ALU.mult))
    em.drain(v)
    em.op(v, lambda e: e.tensor_tensor(out=u3, in0=v3, in1=u3, op=ALU.subtract))
    em.drain(v)
    rkb3 = st[hb, 20:24].rearrange("p (h o) -> p h o", o=1).broadcast_to((64, 4, 256))
    ub3 = ts["s_ub"][hb, :].rearrange("p (h f) -> p h f", f=256)
    em.op(v, lambda e: e.tensor_tensor(out=ub3, in0=u3, in1=rkb3, op=ALU.mult),
          inc="vv")
    em.drain(v)
    em.wait(t, ("vv",))
    for h in range(4):
        hs = slice(h * 64, (h + 1) * 64)
        em.op(t, lambda e, h=h, hs=hs: e.matmul(
            out=ts["p_B"][hb, h * 64:(h + 1) * 64],
            lhsT=_PwSlice(ts, 3, 0)[hb, hs], rhs=_PwSlice(ts, 3, 1)[hb, hs],
            start=True, stop=True, tile_position=tpKK),
            inc="tt" if h == 3 else None)
    em.wait(v, ("tt",))
    em.op(v, lambda e: e.tensor_copy(out=ts["s_Lt"][hb, :], in_=ts["p_B"][hb, 0:256]),
          inc="vv")
    em.drain(v)
    app_seq = [(ts["s_Lt"], 1.0), (_PwSlice(ts, 3, 1), 1.0), (_PwSlice(ts, 2, 1), 1.0),
               (_PwSlice(ts, 1, 1), 1.0), (_PwSlice(ts, 0, 1), 1.0),
               (ts["s_AtT"], -1.0)]
    for lhsT_t, sgn in app_seq:
        em.wait(t, ("vv",))
        for h in range(4):
            em.op(t, lambda e, h=h, lhsT_t=lhsT_t: e.matmul(
                out=ts["p_A"][hb, h * 256:(h + 1) * 256],
                lhsT=lhsT_t[hb, h * 64:(h + 1) * 64],
                rhs=ts["s_ub"][hb, h * 256:(h + 1) * 256],
                start=True, stop=True, tile_position=tpKK),
                inc="tt" if h == 3 else None)
        em.wait(v, ("tt",))
        em.op(v, lambda e, sgn=sgn: e.scalar_tensor_tensor(
            out=ts["s_ub"][hb, :], in0=ts["p_A"][hb, :], scalar=sgn,
            in1=ts["s_ub"][hb, :], op0=ALU.mult, op1=ALU.add), inc="vv")
        em.drain(v)
    em.wait(t, ("vv",))
    for h in range(4):
        qTc = ts["s_qkT"][:, h * 128 + tok0: h * 128 + tok0 + 64]
        em.op(t, lambda e, h=h, qTc=qTc: e.matmul(
            out=ts["p_D"][hb, h * 256:(h + 1) * 256], lhsT=qTc,
            rhs=ts["s_Sbf"][:, h * 256:(h + 1) * 256],
            start=(h % 2 == 0), stop=False, tile_position=tpO))
    for h in range(4):
        qTc = ts["s_qkT"][:, h * 128 + tok0: h * 128 + tok0 + 64]
        kTc = ts["s_qkT"][:, 512 + h * 128 + tok0: 512 + h * 128 + tok0 + 64]
        em.op(t, lambda e, h=h, qTc=qTc, kTc=kTc: e.matmul(
            out=ts["p_C"][hb, h * 64:(h + 1) * 64], lhsT=kTc, rhs=qTc,
            start=True, stop=True, tile_position=tpO),
            inc="tt" if h == 3 else None)
    em.wait(v, ("tt",))
    r3 = ts["p_C"][hb, :].rearrange("p (h f) -> p h f", f=64)
    triu3 = triuI[hb, :].rearrange("p (o f) -> p o f", o=1).broadcast_to((64, 4, 64))
    em.op(v, lambda e: e.tensor_tensor(out=ts["s_Lt"][hb, :].rearrange(
        "p (h f) -> p h f", f=64), in0=r3, in1=triu3, op=ALU.mult), inc="vv")
    em.drain(v)
    em.wait(t, ("vv",))
    for h in range(4):
        em.op(t, lambda e, h=h: e.matmul(
            out=ts["p_D"][hb, h * 256:(h + 1) * 256],
            lhsT=ts["s_Lt"][hb, h * 64:(h + 1) * 64],
            rhs=ts["s_ub"][hb, h * 256:(h + 1) * 256],
            start=False, stop=(h % 2 == 1), tile_position=tpKK),
            inc="tt" if h == 3 else None)
    for h in range(4):
        em.op(t, lambda e, h=h: e.matmul(
            out=ts["p_S"][:, h * 256:(h + 1) * 256],
            lhsT=ts["s_qk"][hb, 512 + h * 128: 512 + (h + 1) * 128],
            rhs=ts["s_ub"][hb, h * 256:(h + 1) * 256],
            start=True, stop=True, tile_position=(64 * c, 0)),
            inc="tt" if h == 3 else None)
    em.wait(v, ("tt",))
    o3 = ts["s_o"][hb, :].rearrange("p (h f) -> p h f", f=256)
    rq3 = st[hb, 12:16].rearrange("p (h o) -> p h o", o=1).broadcast_to((64, 4, 256))
    em.op(v, lambda e: e.tensor_tensor(out=o3, in0=ts["p_D"][hb, :].rearrange(
        "p (h f) -> p h f", f=256), in1=rq3, op=ALU.mult))
    em.op(v, lambda e: e.scalar_tensor_tensor(
        out=ts["s_Sf"][:], in0=ts["p_S"][:], scalar=1.0, in1=ts["s_Sf"][:],
        op0=ALU.mult, op1=ALU.add))
    em.drain(v)
    em.op(v, lambda e: e.tensor_copy(out=ts["s_Sbf"][:], in_=ts["s_Sf"][:]), inc="vv")
    em.drain(v)


class _PwSlice:
    def __init__(self, ts, Lv, which):
        self.t = ts["s_pw"]
        self.base = Lv * 512 + which * 256
    def __getitem__(self, key):
        hb, hs = key
        if isinstance(hs, slice):
            return self.t[hb, self.base + hs.start: self.base + hs.stop]
        raise KeyError(key)



_HOST_CONSTS = None

def _host_consts():
    global _HOST_CONSTS
    if _HOST_CONSTS is None:
        tril = np.tril(np.ones((64, 64), np.float32), -1)
        triu = np.triu(np.ones((64, 64), np.float32), 0)
        ident = np.eye(64, dtype=np.float32)
        id4 = np.concatenate([ident] * 4, axis=1)
        row = np.concatenate([tril, triu, id4], axis=1)
        _HOST_CONSTS = np.concatenate([row, row], axis=0)
    return _HOST_CONSTS


def kernel(**inputs):
    x = np.ascontiguousarray(inputs["L0_attn_hidden_states"], dtype=np.float32)
    consts = _host_consts()
    nw_rep = np.ascontiguousarray(
        np.broadcast_to(inputs["L0_attn_norm_w"][None, :], (128, 256)), np.float32)
    sin_rep = np.ascontiguousarray(
        np.broadcast_to(inputs["L0_ffn_s_in"].reshape(1, 1), (128, 1)), np.float32)
    sid_rep = np.ascontiguousarray(
        np.broadcast_to(inputs["L0_ffn_s_in_down"].reshape(1, 1), (128, 1)), np.float32)
    shared = {
        "Wq": np.ascontiguousarray(inputs["L0_attn_Wq"], np.float32),
        "Wk": np.ascontiguousarray(inputs["L0_attn_Wk"], np.float32),
        "Wv": np.ascontiguousarray(inputs["L0_attn_Wv"], np.float32),
        "Wb": np.ascontiguousarray(inputs["L0_attn_Wb"], np.float32),
        "Wg": np.ascontiguousarray(inputs["L0_attn_Wg"], np.float32),
        "Wo": np.ascontiguousarray(inputs["L0_attn_Wo"], np.float32),
        "conv_w": np.ascontiguousarray(inputs["L0_attn_conv_w"], np.float32),
        "norm_w_rep": nw_rep,

        "s_in_rep": sin_rep,
        "s_in_down_rep": sid_rep,
        "masks": consts,
    }
    in_maps = []
    for c in range(8):
        b, half = c // 2, c % 2
        m = dict(shared)
        m["x_seq"] = np.ascontiguousarray(x[b])
        m["x_own"] = np.ascontiguousarray(x[b, half * HALF:(half + 1) * HALF])
        m["gw_shard"] = np.ascontiguousarray(
            inputs["L0_ffn_gate_w"][c * (DI // 8):(c + 1) * (DI // 8)], np.float32)
        m["uw_shard"] = np.ascontiguousarray(
            inputs["L0_ffn_up_w"][c * (DI // 8):(c + 1) * (DI // 8)], np.float32)
        m["dw_shard"] = np.ascontiguousarray(
            inputs["L0_ffn_down_w"][c * (D // 8):(c + 1) * (D // 8)], np.float32)
        in_maps.append(m)
    nc = build_program()
    res = run_bass_kernel_spmd(nc, in_maps, core_ids=list(range(8)))
    out = np.empty((B, L, D), np.float32)
    for c in range(8):
        b, half = c // 2, c % 2
        out[b, half * HALF:(half + 1) * HALF] = \
            res.results[c]["out_full"][half * HALF:(half + 1) * HALF]
    return out
